# version 14
# baseline (speedup 1.0000x reference)
import math
import numpy as np
import ml_dtypes
import concourse.bass as bass
import concourse.mybir as mybir
from concourse.bass_utils import run_bass_kernel_spmd
from contextlib import ExitStack

F32 = mybir.dt.float32
BF16 = mybir.dt.bfloat16
I32 = mybir.dt.int32
AF = mybir.ActivationFunctionType
ALU = mybir.AluOpType
AX = mybir.AxisListType

T = 2048
D = 1024
NKC = 8
NT = 16
NST = 16
HEADS = 16
DH = 64
DI = 2730
W1C = 5460
NC2 = 22
DEPTH6 = 6
NEGBIG = -30000.0
PI = math.pi
TWO_PI = 2.0 * math.pi
C1 = float(np.float32(TWO_PI))
C2 = float(TWO_PI - C1)
MC = 256
NMC = T // MC
LN_EPS = 1e-5


_TABS_CACHE = [None]


def _jax_tables():
    if _TABS_CACHE[0] is not None:
        return _TABS_CACHE[0]
    import jax
    import jax.numpy as jnp
    with jax.default_device(jax.devices("cpu")[0]):
        half = jnp.arange(0, DH, 2, dtype=jnp.float32)
        inv_freq = 1.0 / 10000.0 ** (half / DH)
        t = jnp.arange(T, dtype=jnp.float32)
        freqs = t[:, None] * inv_freq[None, :]
        freqs = jnp.concatenate([freqs, freqs], -1)
        base_scale = (half + 0.4 * DH) / (1.4 * DH)
        power = (t - T // 2) / (256 // 2)
        scale = base_scale[None, :] ** power[:, None]
        scale = jnp.concatenate([scale, scale], -1)
        cos = np.asarray(jnp.cos(freqs), dtype=np.float32)
        sin = np.asarray(jnp.sin(freqs), dtype=np.float32)
        scl = np.asarray(scale, dtype=np.float32)
        inv = np.asarray(1.0 / scale, dtype=np.float32)
    sinS = np.concatenate([-sin[:, :32], sin[:, 32:]], axis=1)
    _TABS_CACHE[0] = np.ascontiguousarray(
        np.stack([cos, sinS, scl, inv]), dtype=np.float32)
    return _TABS_CACHE[0]


def host_consts():
    ii = np.arange(128)
    c = {}
    c["tabs"] = _jax_tables()
    c["idm"] = np.eye(128, dtype=np.float32)
    c["mku"] = np.where(ii[None, :] < ii[:, None], NEGBIG, 0.0).astype(np.float32)
    c["mkl"] = np.where(ii[None, :] > ii[:, None], NEGBIG, 0.0).astype(np.float32)
    c["mk3"] = np.concatenate(
        [np.where(ii[None, :] < ii[:, None], NEGBIG, 0.0),
         np.zeros((128, 128)),
         np.where(ii[None, :] > ii[:, None], NEGBIG, 0.0)],
        axis=1).astype(np.float32)
    c["ones1"] = np.ones((1, 128), dtype=np.float32)
    c["ones128"] = np.ones((128, 128), dtype=np.float32)
    c["p_col"] = ii.astype(np.float32).reshape(128, 1)
    c["pquot"] = (ii.astype(np.float64) / 128.0).astype(np.float32).reshape(128, 1)
    c["half_row"] = np.arange(0, DH, 2, dtype=np.float32).reshape(1, DH // 2)
    return c


class Bld:
    ENG = ("tensor", "vector", "scalar", "gpsimd", "sync")

    def __init__(self, nc, es):
        self.nc = nc
        self.es = es
        self.q = {e: [] for e in self.ENG}
        self.prog = {}
        self.prog_n = {}
        self.last_wait = {e: {} for e in self.ENG}
        self.nsem = 0
        for e in self.ENG:
            self._new_prog(e)
        self.dsem = {}
        self.dn = {}
        self.dcarry = {}
        self.bar_n = 0
        self.bar_sem = self._sem("bars")

    def _sem(self, name):
        self.nsem += 1
        return self.es.enter_context(self.nc.semaphore(f"{name}{self.nsem}"))

    def _new_prog(self, e):
        self.prog[e] = self._sem(f"p_{e}_")
        self.prog_n[e] = 0

    def op(self, e, fn):
        if self.prog_n[e] >= 28000:
            self._new_prog(e)
        h = self.prog[e]
        self.prog_n[e] += 1
        v = self.prog_n[e]

        def go(eng, fn=fn, h=h):
            fn(eng).then_inc(h, 1)
        self.q[e].append(go)
        return (h, v)

    def now(self, e):
        return (self.prog[e], self.prog_n[e])

    def opf(self, e, fn):
        tok = self.op(e, fn)
        self.wait(e, tok)
        return tok

    def wait(self, e, tok):
        if tok is None:
            return
        if isinstance(tok, list):
            for t in tok:
                self.wait(e, t)
            return
        h, v = tok
        lw = self.last_wait[e]
        if lw.get(id(h), -1) >= v:
            return
        lw[id(h)] = v

        def go(eng, h=h, v=v):
            eng.wait_ge(h, v)
        self.q[e].append(go)

    def dma(self, e, name, fn):
        if name not in self.dsem:
            self.dsem[name] = self._sem(f"d_{name}")
            self.dn[name] = 0
            self.dcarry[name] = []
        elif self.dn[name] >= 29000:
            self.dcarry[name] = self.dcarry.get(name, []) + [
                (self.dsem[name], self.dn[name])]
            self.dsem[name] = self._sem(f"d_{name}")
            self.dn[name] = 0
        h = self.dsem[name]
        self.dn[name] += 16
        v = self.dn[name]

        def go(eng, fn=fn, h=h):
            fn(eng).then_inc(h, 16)
        self.q[e].append(go)
        return [(h, v)] + self.dcarry.get(name, [])

    def dtok(self, name):
        return [(self.dsem[name], self.dn[name])] + self.dcarry.get(name, [])

    def barrier(self):
        bar = self.bar_sem
        n = len(self.ENG)
        self.bar_n += n
        tgt = self.bar_n
        for e in self.ENG:
            def go(eng, bar=bar):
                eng.nop().then_inc(bar, 1)
            self.q[e].append(go)
        for e in self.ENG:
            def go2(eng, bar=bar, v=tgt):
                eng.wait_ge(bar, v)
            self.q[e].append(go2)
        if self.bar_n > 28000:
            self.bar_sem = self._sem("bars")
            self.bar_n = 0

    def finalize(self):
        with self.nc.Block() as block:
            for e in self.ENG:
                qe = self.q[e]

                def runner(eng, qe=qe):
                    for fn in qe:
                        fn(eng)
                getattr(block, e)(runner)


def build_program(debug=()):
    nc = bass.Bass("TRN2", target_bir_lowering=False,
                   detect_race_conditions=False)
    es = ExitStack()
    dbg_set = set(debug)
    dbgt = {}

    def din(name, shape):
        return nc.dram_tensor(name, list(shape), F32, kind="ExternalInput").ap()

    x_d = din("x", (T, D))
    A_d = din("A", (D, NST))
    W_B_d = din("W_B", (NST, D))
    b_B_d = din("b_B", (NST,))
    W_C_d = din("W_C", (NST, D))
    b_C_d = din("b_C", (NST,))
    W_Dl_d = din("W_Delta", (1, D))
    b_Dl_d = din("b_Delta", (1,))
    bias_Dl_d = din("bias_Delta", (1,))
    Wqkv_d = din("Wqkv", (DEPTH6, D, 3 * D))
    qs_d = din("q_scale", (DEPTH6, DH))
    ks_d = din("k_scale", (DEPTH6, DH))
    Wo_d = din("Wo", (DEPTH6, D, D))
    lng_d = din("ln_g", (DEPTH6, D))
    lnb_d = din("ln_b", (DEPTH6, D))
    W1_d = din("W1", (DEPTH6, D, W1C))
    b1_d = din("b1", (DEPTH6, W1C))
    W2_d = din("W2", (DEPTH6, DI, D))
    b2_d = din("b2", (DEPTH6, D))
    HCv = host_consts()
    HC = {k2: din("c_" + k2, v.shape) for k2, v in HCv.items()}

    out_d = nc.dram_tensor("out", [T, D], F32, kind="ExternalOutput").ap()
    xn_scr = nc.dram_tensor("xn_scr", [NKC, 128, T], BF16, kind="Internal").ap()
    y_scr = nc.dram_tensor("y_scr", [NT, 128, D], F32, kind="Internal").ap()
    gg_scr = nc.dram_tensor("gg_scr", [NC2, 128, T], BF16, kind="Internal").ap()
    bmd_scr = nc.dram_tensor("bmd_scr", [17, T], F32, kind="Internal").ap()

    def dbg(name, shape):
        if name in dbg_set:
            dbgt[name] = nc.dram_tensor("dbg_" + name, list(shape), F32,
                                        kind="ExternalOutput").ap()
            return dbgt[name]
        return None

    k = Bld(nc, es)
    sb = lambda name, shape, dt: es.enter_context(nc.sbuf_tensor(name, shape, dt))

    h = [sb(f"h{i}", [128, 2050], F32) for i in range(NKC)]
    regAb = sb("regAb", [128, 24576], BF16)
    wqkv_v = regAb.ap()[:, 0:24576].rearrange("p (kk n) -> p kk n", kk=NKC)
    w2_v = regAb.ap()[:, 0:22528].rearrange("p (c n) -> p c n", c=NC2)
    mv_xn = regAb.ap()[:, 0:2048].rearrange("p (kk t) -> p kk t", kk=NKC)
    mv_h = regAb.ap()[:, 2048:4096].rearrange("p (kk t) -> p kk t", kk=NKC)
    mv_hT = regAb.ap()[:, 4096:6144].rearrange("p (j d) -> p j d", j=MC // 128)
    regBb = sb("regBb", [128, 8192], BF16)
    wo_v = regBb.ap()[:, 0:8192].rearrange("p (kk n) -> p kk n", kk=NKC)
    xn2_v = regBb.ap()[:, 0:8192].rearrange("p (kk t) -> p kk t", kk=NKC)
    scrF = sb("scrF", [128, 5376], F32)
    io_a = scrF.ap()[:, 0:1024]
    io_b = scrF.ap()[:, 1024:2048]
    io_c = scrF.ap()[:, 2048:3072]
    mv_bmdf = scrF.ap()[:, 0:2048]
    mv_db = scrF.ap()[:, 2048:2304]
    mv_bm = scrF.ap()[:, 2304:2560]
    mv_ab = [scrF.ap()[:, 2560:2816], scrF.ap()[:, 0:256]]
    mv_wt = [scrF.ap()[:, 2816:3072], scrF.ap()[:, 256:512]]
    mv_bx = [scrF.ap()[:, 3072:3328], scrF.ap()[:, 512:768]]
    mv_xb = [scrF.ap()[:, 768:1024], scrF.ap()[:, 1024:1280]]
    mv_y = scrF.ap()[:, 3328:5376].rearrange("p (j d) -> p j d", j=MC // 128)

    id_f = sb("id_f", [128, 128], F32)
    id_b = sb("id_b", [128, 128], BF16)
    mk3_b = sb("mk3", [128, 384], BF16)
    ones1_f = sb("ones1_f", [1, 128], F32)
    ones1_b = sb("ones1_b", [1, 128], BF16)
    ones128_b = sb("ones128_b", [128, 128], BF16)
    p_col = sb("p_col", [128, 1], F32)
    pquot = sb("pquot", [128, 1], F32)
    half_row = sb("half_row", [1, DH // 2], F32)
    eps_col = sb("eps_col", [128, 1], F32)
    one_col = sb("one_col", [128, 1], F32)
    tcmb = sb("tcmb", [128, DH], F32)
    tab4 = sb("tab4", [128, 4, DH], F32)
    qcs = sb("qcs", [128, DH], F32)
    qss = sb("qss", [128, DH], F32)
    kcs = sb("kcs", [128, DH], F32)
    kss = sb("kss", [128, DH], F32)
    qs_rep = sb("qs_rep", [128, DH], F32)
    ks_rep = sb("ks_rep", [128, DH], F32)
    qs_swp = sb("qs_swp", [128, DH], F32)
    ks_swp = sb("ks_swp", [128, DH], F32)
    lng_c = sb("lng_c", [128, NKC], F32)
    lnb_c = sb("lnb_c", [128, NKC], F32)
    b1_c = sb("b1_c", [128, 44], F32)
    b2_c = sb("b2_c", [128, NKC], F32)
    small_st = sb("small_st", [48, 128], F32)
    qsrow = sb("qsrow", [1, DH], F32)
    ksrow = sb("ksrow", [1, DH], F32)
    A_f = sb("A_f", [128, NKC, NST], F32)
    invA_f = sb("invA_f", [128, NKC, NST], F32)
    WBCD = sb("WBCD", [128, NKC, 33], BF16)
    xnb_st = sb("xnb_st", [128, 128], BF16)
    brow_f = sb("brow_f", [1, 34], F32)
    brow_b = sb("brow_b", [1, 33], BF16)
    BCD = sb("BCD", [128, NT, 33], F32)
    st_cols = sb("st_cols", [128, NST, NKC], F32)
    sm1 = sb("sm1", [128, NT], F32)
    sm2 = sb("sm2", [128, NT], F32)
    xsh = [sb(f"xsh{i}", [128, NKC, 128], BF16) for i in range(2)]
    hbnd = scrF.ap()[:, 3328:3392].rearrange("p (a b) -> p a b", a=4)
    q_tok = sb("q_tok", [128, 1024], F32)
    k_tok = sb("k_tok", [128, 1024], F32)
    qsq = sb("qsq", [128, 1024], BF16)
    ksq = sb("ksq", [128, 1024], BF16)
    qp2 = qsq
    kp2 = ksq
    qrot = sb("qrot", [128, 1024], BF16)
    krot = sb("krot", [128, 1024], BF16)
    q_fT = sb("q_fT", [128, NKC, 128], BF16)
    kring = [sb(f"kring{i}", [128, NKC, 128], BF16) for i in range(3)]
    vring = [sb(f"vring{i}", [128, 1024], BF16) for i in range(3)]
    rsq8 = sb("rsq8", [128, HEADS], F32)
    nmax = sb("nmax", [128, HEADS], F32)
    bias_h = sb("bias_h", [128, HEADS], F32)
    lsum = sb("lsum", [128, HEADS], F32)
    linv = sb("linv", [128, HEADS], F32)
    ssc = sb("ssc", [128, HEADS], F32)
    P_sb = [sb(f"P{i}", [128, 384], BF16) for i in range(2)]
    PT_sb = [sb(f"PT{i}", [128, 3, 128], BF16) for i in range(2)]
    ctx_tok = sb("ctx_tok", [128, 1024], BF16)
    ctx_fT = sb("ctx_fT", [128, NKC, 128], BF16)
    _w1a = sb("w1a", [128, NKC, 128], BF16)
    _w1g = sb("w1g", [128, NKC, 128], BF16)
    w1a_s = [_w1a, _w1a]
    w1g_s = [_w1g, _w1g]
    gelu_t = q_tok
    ggs = [ctx_tok, krot]
    gg_in = [qrot.ap()[:, 0:256], qrot.ap()[:, 256:512]]
    stat_a = k_tok.ap()[:, 0:256]
    stat_c = k_tok.ap()[:, 256:512]
    stat_t = k_tok.ap()[:, 512:768]
    sq_b = sb("sq_b", [128, 256], BF16)
    xh_b = sb("xh_b", [128, 256], BF16)
    ps0 = es.enter_context(nc.psum_tensor("ps0", [128, 2048], F32))
    ps1 = es.enter_context(nc.psum_tensor("ps1", [128, 1024], F32))
    ps2 = es.enter_context(nc.psum_tensor("ps2", [128, 512], F32))
    ps3 = es.enter_context(nc.psum_tensor("ps3", [128, 512], F32))

    def hsl(kk, t0, n, shifted):
        off = 1 + t0 - (1 if (shifted and kk >= 4) else 0)
        return h[kk][:, off:off + n]

    def hb(t):
        return t.ap().unsqueeze(1).broadcast_to([128, HEADS, DH])

    def hb2(t, half):
        v = t.ap().rearrange("p (two d) -> p two d", two=2)[:, half:half + 1, :]
        return v.unsqueeze(1).broadcast_to([128, HEADS, 1, 32])

    def v3(t):
        return t.ap().rearrange("p (g d) -> p g d", g=HEADS)

    def v4(t):
        return t.ap().rearrange("p (g two d) -> p g two d", g=HEADS, two=2)

    for t_sb, cname, eng in [
        (id_f, "idm", "sync"), (id_b, "idm", "gpsimd"),
        (mk3_b, "mk3", "gpsimd"),
        (ones1_f, "ones1", "sync"), (ones1_b, "ones1", "gpsimd"),
        (ones128_b, "ones128", "gpsimd"), (p_col, "p_col", "sync"),
        (pquot, "pquot", "sync"), (half_row, "half_row", "sync"),
    ]:
        k.dma(eng, "cst_" + eng, (lambda e, t=t_sb, dd=HC[cname]:
                                  e.dma_start(t[:, :], dd)))
    for e in Bld.ENG:
        k.wait(e, k.dtok("cst_sync"))
        k.wait(e, k.dtok("cst_gpsimd"))
    k.opf("vector", lambda e: e.memset(eps_col[:, :], 0.0))
    k.opf("vector", lambda e: e.tensor_scalar(eps_col[:, :], eps_col[:, :], LN_EPS,
                                              None, ALU.add))
    k.opf("vector", lambda e: e.memset(one_col[:, :], 0.0))
    k.opf("vector", lambda e: e.tensor_scalar(one_col[:, :], one_col[:, :], 1.0,
                                              None, ALU.add))
    for i in range(NKC):
        k.op("vector", lambda e, i=i: e.memset(h[i][:, 0:1], 0.0))
    k.op("vector", lambda e: e.memset(st_cols[:, :, :], 0.0))
    k.barrier()

    for tau in range(NT):
        ld = k.dma("gpsimd", "xin", (lambda e, tau=tau: e.dma_start(
            io_a[:, :], x_d[128 * tau:128 * (tau + 1), :])))
        k.wait("vector", ld)
        k.opf("vector", lambda e: e.reduce_sum(ssc[:, 0:1], io_a[:, :], axis=AX.X))
        k.opf("vector", lambda e: e.tensor_scalar(ssc[:, 1:2], ssc[:, 0:1],
                                                  1.0 / D, None, ALU.mult))
        k.op("vector", lambda e: e.tensor_scalar(io_b[:, :], io_a[:, :],
                                                 ssc[:, 1:2], None, ALU.subtract))
        sqd = k.op("vector", lambda e: e.tensor_mul(io_c[:, :], io_b[:, :],
                                                    io_b[:, :]))
        k.wait("gpsimd", sqd)
        vs = k.op("vector", lambda e: e.reduce_sum(ssc[:, 2:3], io_c[:, :],
                                                   axis=AX.X))
        k.wait("scalar", vs)
        k.op("scalar", lambda e: e.activation(ssc[:, 3:4], ssc[:, 2:3], AF.Ln,
                                              bias=eps_col[:, :], scale=1.0 / D))
        s2 = k.op("scalar", lambda e: e.activation(ssc[:, 4:5], ssc[:, 3:4],
                                                   AF.Exp, scale=-0.5))
        k.wait("vector", s2)
        xn_done = k.op("vector", lambda e: e.tensor_scalar(
            io_b[:, :], io_b[:, :], ssc[:, 4:5], None, ALU.mult))
        k.wait("tensor", xn_done)
        for half in range(2):
            tt = None
            for j in range(4):
                c = 4 * half + j
                tt = k.op("tensor", lambda e, c=c, j=j: e.transpose(
                    ps0[:, 512 * j:512 * j + 128], io_b[:, 128 * c:128 * (c + 1)],
                    id_f[:, :]))
            k.wait("vector", tt)
            ev = None
            for j in range(4):
                c = 4 * half + j
                ev = k.op("vector", lambda e, c=c, j=j, tau=tau: e.tensor_copy(
                    h[c][:, 1 + 128 * tau:1 + 128 * (tau + 1)],
                    ps0[:, 512 * j:512 * j + 128]))
            k.wait("tensor", ev)
    htok = k.now("vector")
    for i in range(NKC):
        k.wait("gpsimd", htok)
        k.dma("gpsimd", "xnst", (lambda e, i=i: e.dma_start(
            xn_scr[i, :, :], h[i][:, 1:1 + T])))
    d = dbg("xn", (NKC * 128, T))
    if d is not None:
        for i in range(NKC):
            k.wait("sync", htok)
            k.dma("sync", "dbgd", (lambda e, i=i, d=d: e.dma_start(
                d[128 * i:128 * (i + 1), :], h[i][:, 1:1 + T])))
    k.barrier()

    for dt in range(NKC):
        k.dma("gpsimd", "msm", (lambda e, dt=dt: e.dma_start(
            A_f[:, dt, :], A_d[128 * dt:128 * (dt + 1), :])))
    k.wait("vector", k.dtok("msm"))
    for dt in range(NKC):
        k.op("vector", lambda e, dt=dt: e.reciprocal(invA_f[:, dt, :],
                                                     A_f[:, dt, :]))
    for kk in range(NKC):
        k.dma("sync", "msm2", (lambda e, kk=kk: e.dma_start(
            small_st[0:16, :], W_B_d[:, 128 * kk:128 * (kk + 1)])))
        k.dma("sync", "msm2", (lambda e, kk=kk: e.dma_start(
            small_st[16:32, :], W_C_d[:, 128 * kk:128 * (kk + 1)])))
        l3 = k.dma("sync", "msm2", (lambda e, kk=kk: e.dma_start(
            small_st[32:33, :], W_Dl_d[:, 128 * kk:128 * (kk + 1)])))
        k.wait("tensor", l3)
        tr = k.op("tensor", lambda e: e.transpose(
            ps2[0:128, 0:33], small_st[0:33, :], id_f[0:33, 0:33]))
        k.wait("vector", tr)
        ev = k.op("vector", lambda e, kk=kk: e.tensor_copy(WBCD[:, kk, :],
                                                           ps2[:, 0:33]))
        k.wait("tensor", ev)
        k.wait("sync", ev)
    k.dma("sync", "msm2", lambda e: e.dma_start(brow_f[0:1, 0:16], b_B_d))
    k.dma("sync", "msm2", lambda e: e.dma_start(brow_f[0:1, 16:32], b_C_d))
    k.dma("sync", "msm2", lambda e: e.dma_start(brow_f[0:1, 32:33], b_Dl_d))
    b4l = k.dma("sync", "msm2", lambda e: e.dma_start(brow_f[0:1, 33:34],
                                                      bias_Dl_d))
    k.wait("vector", b4l)
    k.opf("vector", lambda e: e.tensor_add(brow_f[0:1, 32:33], brow_f[0:1, 32:33],
                                           brow_f[0:1, 33:34]))
    bb = k.op("vector", lambda e: e.tensor_copy(brow_b[0:1, :],
                                                brow_f[0:1, 0:33]))
    k.wait("tensor", bb)
    for tau in range(NT):
        for kk in range(NKC):
            cp = k.op("vector", lambda e, kk=kk, tau=tau: e.tensor_copy(
                xnb_st[:, :], hsl(kk, 128 * tau, 128, False)))
            k.wait("tensor", cp)
            mm = k.op("tensor", lambda e, kk=kk: e.matmul(
                ps2[:, 0:33], xnb_st[:, :], WBCD[:, kk, :],
                start=(kk == 0), stop=False))
            k.wait("vector", mm)
        bm = k.op("tensor", lambda e: e.matmul(
            ps2[:, 0:33], ones1_b[:, :], brow_b[:, :], start=False, stop=True))
        k.wait("scalar", bm)
        ev = k.op("scalar", lambda e, tau=tau: e.copy(BCD[:, tau, :],
                                                      ps2[:, 0:33]))
        k.wait("tensor", ev)
    dcol = BCD.ap()[:, :, 32:33].squeeze(2)
    k.wait("scalar", k.now("scalar"))
    k.op("scalar", lambda e: e.activation(sm1[:, :], dcol, AF.Abs))
    k.op("scalar", lambda e: e.activation(sm1[:, :], sm1[:, :], AF.Exp,
                                          scale=-1.0))
    t1 = k.op("scalar", lambda e: e.activation(sm1[:, :], sm1[:, :], AF.Ln,
                                               bias=one_col[:, :]))
    k.wait("vector", t1)
    k.opf("vector", lambda e: e.tensor_scalar(sm2[:, :], dcol, 0.0, None, ALU.max))
    spd = k.op("vector", lambda e: e.tensor_tensor(dcol, sm1[:, :], sm2[:, :],
                                                   ALU.add))
    k.wait("tensor", spd)
    for tau in range(NT):
        k.op("tensor", lambda e, tau=tau: e.transpose(
            ps2[0:16, 0:128], BCD.ap()[:, tau, 0:16], id_f[:, :]))
        t2 = k.op("tensor", lambda e, tau=tau: e.transpose(
            ps2[0:1, 128:256], BCD.ap()[:, tau, 32:33], id_f[:, :]))
        k.wait("vector", t2)
        k.op("vector", lambda e, tau=tau: e.tensor_copy(
            mv_bmdf[0:16, 128 * tau:128 * (tau + 1)], ps2[0:16, 0:128]))
        ev = k.op("vector", lambda e, tau=tau: e.tensor_copy(
            mv_bmdf[32:33, 128 * tau:128 * (tau + 1)], ps2[0:1, 128:256]))
        k.wait("tensor", ev)
    bmdv = k.now("vector")
    k.wait("sync", bmdv)
    k.dma("sync", "bmdst", (lambda e: e.dma_start(bmd_scr[0:16, :],
                                                  mv_bmdf[0:16, 0:T])))
    k.dma("sync", "bmdst", (lambda e: e.dma_start(bmd_scr[16:17, :],
                                                  mv_bmdf[32:33, 0:T])))
    d = dbg("bcd", (128, NT * 33))
    if d is not None:
        tok = k.now("vector")
        k.wait("sync", tok)
        k.dma("sync", "dbgd", (lambda e, d=d: e.dma_start(
            d, BCD.ap().rearrange("p a b -> p (a b)"))))
    k.barrier()

    scan_guard = {}
    mtr_guard = None
    k.wait("scalar", k.dtok("msm"))
    gp_prev = None
    for c in range(NMC):
        cc = slice(MC * c, MC * (c + 1))
        if gp_prev is not None:
            k.wait("sync", gp_prev)
        k.wait("sync", k.dtok("xnst"))
        xl = k.dma("sync", "mxn", (lambda e, cc=cc: e.dma_start(
            mv_xn, xn_scr[:, :, cc].transpose([1, 0, 2]))))
        k.op("vector", lambda e: e.memset(mv_y[:, :, :], 0.0))
        k.wait("sync", k.dtok("bmdst"))
        k.wait("sync", k.now("scalar"))
        dbl = k.dma("sync", "mbc", (lambda e, cc=cc: e.dma_start(
            mv_db, bmd_scr[16:17, cc].broadcast_to([128, MC]))))
        k.wait("scalar", dbl)
        k.wait("gpsimd", xl)
        for n in range(NST):
            k.wait("sync", k.now("gpsimd"))
            bml = k.dma("sync", "mbc", (lambda e, n=n, cc=cc: e.dma_start(
                mv_bm, bmd_scr[n:n + 1, cc].broadcast_to([128, MC]))))
            k.wait("gpsimd", bml)
            for dt in range(NKC):
                pp = dt % 2
                gidx = (c * NST + n) * NKC + dt
                if dt == 0:
                    vdrain = k.now("vector")
                    k.wait("scalar", vdrain)
                    k.wait("gpsimd", vdrain)
                elif gidx - 2 in scan_guard:
                    k.wait("scalar", scan_guard[gidx - 2])
                ab_t = k.op("scalar", lambda e, dt=dt, n=n, pp=pp:
                            e.activation(
                    mv_ab[pp], mv_db[:, :], AF.Exp,
                    scale=A_f.ap()[:, dt, n:n + 1]))
                k.wait("vector", ab_t)
                wt_t = k.op("vector", lambda e, dt=dt, n=n, pp=pp:
                            e.tensor_scalar(
                    mv_wt[pp], mv_ab[pp], 1.0,
                    invA_f.ap()[:, dt, n:n + 1],
                    ALU.subtract, ALU.mult))
                if dt >= 2:
                    k.wait("gpsimd", scan_guard[gidx - 2])
                bx_t = k.op("gpsimd", lambda e, dt=dt, pp=pp: e.tensor_tensor(
                    mv_bx[pp], mv_bm[:, :], mv_xn[:, dt, :], ALU.mult))
                k.wait("vector", wt_t)
                k.wait("vector", bx_t)
                k.op("vector", lambda e, pp=pp: e.tensor_tensor(
                    mv_xb[pp], mv_wt[pp], mv_bx[pp], ALU.mult))
                sc_t = k.op("vector", lambda e, dt=dt, n=n, pp=pp:
                            e.tensor_tensor_scan(
                    mv_h[:, dt, :], mv_ab[pp], mv_xb[pp],
                    st_cols.ap()[:, n, dt:dt + 1], ALU.mult, ALU.add))
                scan_guard[gidx] = sc_t
                k.wait("vector", sc_t)
                k.op("vector", lambda e, dt=dt, n=n: e.tensor_copy(
                    st_cols[:, n, dt:dt + 1], mv_h[:, dt, MC - 1:MC]))
            vtok = k.now("vector")
            k.wait("tensor", vtok)
            if mtr_guard is not None:
                k.wait("tensor", mtr_guard)
            ttok = None
            for dt in range(NKC):
                for j in range(MC // 128):
                    ttok = k.op("tensor", lambda e, dt=dt, j=j: e.matmul(
                        ps0[:, 1024 * j + 128 * dt:1024 * j + 128 * (dt + 1)],
                        mv_h[:, dt, 128 * j:128 * (j + 1)], id_b[:, :],
                        start=True, stop=True))
            k.wait("scalar", ttok)
            ev0 = None
            for j in range(MC // 128):
                ev0 = k.op("scalar", lambda e, j=j: e.copy(
                    mv_hT[:, j, :], ps0[:, 1024 * j:1024 * (j + 1)]))
            mtr_guard = ev0
            k.wait("vector", ev0)
            ylast = None
            for j in range(MC // 128):
                tau_g = (MC * c) // 128 + j
                ylast = k.op("vector", lambda e, j=j, n=n, tau_g=tau_g:
                             e.scalar_tensor_tensor(
                    mv_y[:, j, :], mv_hT[:, j, :],
                    BCD.ap()[:, tau_g, 16 + n:17 + n], mv_y[:, j, :],
                    ALU.mult, ALU.add))
        gp_prev = k.now("gpsimd")
        k.wait("sync", k.now("vector"))
        k.wait("sync", gp_prev)
        st1 = None
        for j in range(MC // 128):
            tau_g = (MC * c) // 128 + j
            st1 = k.dma("sync", "yst", (lambda e, j=j, tau_g=tau_g: e.dma_start(
                y_scr[tau_g, :, :], mv_y[:, j, :])))
        k.wait("vector", st1)
        k.wait("gpsimd", st1)
    d = dbg("stc", (128, NST * NKC))
    if d is not None:
        stok = k.now("vector")
        k.wait("sync", stok)
        sdmp = k.dma("sync", "dbgd", (lambda e, d=d: e.dma_start(
            d[:, :], st_cols.ap().rearrange("p a b -> p (a b)"))))
        k.wait("vector", sdmp)
    d = dbg("y", (T, D))
    if d is not None:
        k.wait("sync", k.dtok("yst"))
        for tau_g in range(NT):
            ldy = k.dma("sync", "dbgy", (lambda e, tau_g=tau_g: e.dma_start(
                io_a[:, :], y_scr[tau_g, :, :])))
            k.wait("sync", ldy)
            st9 = k.dma("sync", "dbgd", (lambda e, tau_g=tau_g, d=d: e.dma_start(
                d[128 * tau_g:128 * (tau_g + 1), :], io_a[:, :])))
            k.wait("sync", st9)
    k.barrier()

    d_att = dbg("att2", (8 * 128, 1024)) if "att2" in dbg_set else None
    for L in range(DEPTH6):
        for kk in range(NKC):
            k.dma("gpsimd", "wq", (lambda e, kk=kk, L=L: e.dma_start(
                wqkv_v[:, kk, :], Wqkv_d[L, 128 * kk:128 * (kk + 1), :])))
            k.dma("gpsimd", "wo", (lambda e, kk=kk, L=L: e.dma_start(
                wo_v[:, kk, :], Wo_d[L, 128 * kk:128 * (kk + 1), :])))
        wq_tok = k.dtok("wq")
        wo_tok = k.dtok("wo")
        k.dma("sync", "lsm", (lambda e, L=L: e.dma_start(
            small_st[0:8, :], lng_d[L].rearrange("(a b) -> a b", b=128))))
        k.dma("sync", "lsm", (lambda e, L=L: e.dma_start(
            small_st[8:16, :], lnb_d[L].rearrange("(a b) -> a b", b=128))))
        l3 = k.dma("sync", "lsm", (lambda e, L=L: e.dma_start(
            small_st[16:24, :], b2_d[L].rearrange("(a b) -> a b", b=128))))
        k.wait("tensor", l3)
        tr = k.op("tensor", lambda e: e.transpose(ps2[0:128, 0:24],
                                                  small_st[0:24, :],
                                                  id_f[0:24, 0:24]))
        k.wait("vector", tr)
        k.op("vector", lambda e: e.tensor_copy(lng_c[:, :], ps2[:, 0:8]))
        k.op("vector", lambda e: e.tensor_copy(lnb_c[:, :], ps2[:, 8:16]))
        ev = k.op("vector", lambda e: e.tensor_copy(b2_c[:, :], ps2[:, 16:24]))
        k.wait("tensor", ev)
        k.wait("sync", ev)
        for half, base in ((0, 0), (1, DI)):
            k.wait("vector", k.now("tensor"))
            ms = k.op("vector", lambda e: e.memset(small_st[0:32, :], 0.0))
            k.wait("sync", ms)
            k.dma("sync", "lsm", (lambda e, L=L, base=base: e.dma_start(
                small_st[0:21, :],
                b1_d[L, base:base + 21 * 128].rearrange("(a b) -> a b", b=128))))
            lb_ = k.dma("sync", "lsm", (lambda e, L=L, base=base: e.dma_start(
                small_st[21:22, 0:42], b1_d[L, base + 21 * 128:base + DI])))
            k.wait("tensor", lb_)
            tr = k.op("tensor", lambda e: e.transpose(
                ps2[0:128, 0:22], small_st[0:22, :], id_f[0:22, 0:22]))
            k.wait("vector", tr)
            ev = k.op("vector", lambda e, half=half: e.tensor_copy(
                b1_c[:, 22 * half:22 * (half + 1)], ps2[:, 0:22]))
            k.wait("tensor", ev)
            k.wait("sync", ev)
        lq = k.dma("sync", "lsm", (lambda e, L=L: e.dma_start(
            qs_rep[:, :], qs_d[L:L + 1, :].broadcast_to([128, DH]))))
        lk = k.dma("sync", "lsm", (lambda e, L=L: e.dma_start(
            ks_rep[:, :], ks_d[L:L + 1, :].broadcast_to([128, DH]))))
        k.wait("vector", lk)
        for hl in range(2):
            dsl = slice(32 * hl, 32 * (hl + 1))
            ssl = slice(32 * (1 - hl), 32 * (2 - hl))
            k.op("vector", lambda e, dsl=dsl, ssl=ssl: e.tensor_copy(
                qs_swp[:, dsl], qs_rep[:, ssl]))
            k.op("vector", lambda e, dsl=dsl, ssl=ssl: e.tensor_copy(
                ks_swp[:, dsl], ks_rep[:, ssl]))
        ev = k.now("vector")
        k.wait("tensor", ev)
        k.barrier()

        for kk in range(4, NKC):
            k.op("vector", lambda e, kk=kk: e.tensor_copy(
                hbnd[:, kk - 4, 0:15],
                bass.AP(h[kk], 128, [[2050, 128], [128, 15]])))
        hbnd_tok = k.now("vector")
        prev_t = None
        prev_v = None
        prev_ev = None
        prev_tab_rd = None
        prev_s = None
        for tau in range(NT):
            sl = tau % 2
            rs = tau % 3
            if tau >= 2:
                k.wait("vector", xsh_guard[tau - 2])
            for kk in range(NKC):
                if kk < 4 or tau == 0:
                    k.op("vector", lambda e, kk=kk, sl=sl, tau=tau: e.tensor_copy(
                        xsh[sl][:, kk, :], hsl(kk, 128 * tau, 128, True)))
                else:
                    k.op("vector", lambda e, kk=kk, sl=sl, tau=tau: e.tensor_copy(
                        xsh[sl][:, kk, 0:1], hbnd[:, kk - 4, tau - 1:tau]))
                    k.op("vector", lambda e, kk=kk, sl=sl, tau=tau: e.tensor_copy(
                        xsh[sl][:, kk, 1:128],
                        h[kk][:, 128 * tau + 1:128 * (tau + 1)]))
            xsh_tok = k.now("vector")
            if tau == 0:
                xsh_guard = {}
            k.wait("tensor", xsh_tok)
            k.wait("tensor", wq_tok)
            if prev_v is not None:
                k.wait("tensor", prev_v)
            if prev_ev is not None:
                k.wait("tensor", prev_ev)
            qkv_last = None
            for n in range(6):
                dst = (ps0.ap()[:, 512 * n:512 * (n + 1)] if n < 4 else
                       ps1.ap()[:, 512 * (n - 4):512 * (n - 3)])
                for kk in range(NKC):
                    qkv_last = k.op("tensor", lambda e, kk=kk, sl=sl, dst=dst,
                                    n=n: e.matmul(
                        dst, xsh[sl][:, kk, :],
                        wqkv_v[:, kk, 512 * n:512 * (n + 1)],
                        start=(kk == 0), stop=(kk == NKC - 1)))
            xsh_guard[tau] = qkv_last
            k.wait("scalar", qkv_last)
            if prev_t is not None:
                k.wait("scalar", prev_t)
            k.op("scalar", lambda e: e.copy(q_tok[:, :], ps0[:, 0:1024]))
            k.op("scalar", lambda e: e.copy(k_tok[:, :], ps0[:, 1024:2048]))
            vev = k.op("scalar", lambda e, rs=rs: e.copy(vring[rs][:, :],
                                                         ps1[:, 0:1024]))
            k.wait("tensor", vev)
            if d_att is not None and L == 0 and tau == 2:
                qk_ev2 = k.now("scalar")
                k.wait("sync", qk_ev2)
                dtk_ = k.dma("sync", "dbgd", (lambda e: e.dma_start(
                    d_att[896:1024, 0:1024], q_tok[:, :])))
                k.wait("scalar", dtk_)
                k.wait("vector", dtk_)
            if prev_tab_rd is not None:
                k.wait("sync", prev_tab_rd)
            tl = k.dma("sync", "tab", (lambda e, tau=tau: e.dma_start(
                tab4[:, :, :],
                HC["tabs"][:, 128 * tau:128 * (tau + 1), :].transpose([1, 0, 2]))))
            k.wait("vector", tl)
            k.op("vector", lambda e: e.tensor_tensor(tcmb[:, :], tab4[:, 0, :],
                                                     tab4[:, 2, :], ALU.mult))
            k.op("vector", lambda e: e.tensor_tensor(qcs[:, :], tcmb[:, :],
                                                     qs_rep[:, :], ALU.mult))
            k.op("vector", lambda e: e.tensor_tensor(tcmb[:, :], tab4[:, 1, :],
                                                     tab4[:, 2, :], ALU.mult))
            k.op("vector", lambda e: e.tensor_tensor(qss[:, :], tcmb[:, :],
                                                     qs_swp[:, :], ALU.mult))
            k.op("vector", lambda e: e.tensor_tensor(tcmb[:, :], tab4[:, 0, :],
                                                     tab4[:, 3, :], ALU.mult))
            k.op("vector", lambda e: e.tensor_tensor(kcs[:, :], tcmb[:, :],
                                                     ks_rep[:, :], ALU.mult))
            k.op("vector", lambda e: e.tensor_tensor(tcmb[:, :], tab4[:, 1, :],
                                                     tab4[:, 3, :], ALU.mult))
            tbk = k.op("vector", lambda e: e.tensor_tensor(
                kss[:, :], tcmb[:, :], ks_swp[:, :], ALU.mult))
            prev_tab_rd = tbk
            sq = k.op("scalar", lambda e: e.activation(qsq[:, :], q_tok[:, :],
                                                       AF.Square))
            k.wait("vector", sq)
            k.opf("vector", lambda e: e.reduce_sum(
                ssc.ap().rearrange("p (a b) -> p a b", b=1),
                qsq.ap().rearrange("p (g d) -> p g d", d=DH), axis=AX.X))
            rq = k.op("vector", lambda e: e.tensor_scalar(
                ssc[:, :], ssc[:, :], 1e-24, None, ALU.max))
            k.wait("scalar", rq)
            k.op("scalar", lambda e: e.activation(ssc[:, :], ssc[:, :], AF.Ln))
            rqe = k.op("scalar", lambda e: e.activation(rsq8[:, :], ssc[:, :],
                                                        AF.Exp, scale=-0.5))
            k.wait("vector", rqe)
            k.opf("vector", lambda e: e.tensor_scalar(rsq8[:, :], rsq8[:, :],
                                                      8.0, None, ALU.mult))
            k.op("vector", lambda e: e.tensor_tensor(
                v3(q_tok), v3(q_tok),
                rsq8.ap().unsqueeze(2).broadcast_to([128, HEADS, DH]),
                ALU.mult))
            k.op("vector", lambda e: e.tensor_tensor(
                v3(qp2), v3(q_tok), hb(qcs), ALU.mult))
            k.op("vector", lambda e: e.tensor_tensor(
                v4(qrot)[:, :, 0:1, :], v4(q_tok)[:, :, 1:2, :],
                hb2(qss, 0), ALU.mult))
            k.op("vector", lambda e: e.tensor_tensor(
                v4(qrot)[:, :, 1:2, :], v4(q_tok)[:, :, 0:1, :],
                hb2(qss, 1), ALU.mult))
            qdone = k.op("vector", lambda e: e.tensor_tensor(
                qrot[:, :], qrot[:, :], qp2[:, :], ALU.add))
            ks_ = k.op("scalar", lambda e: e.activation(ksq[:, :], k_tok[:, :],
                                                        AF.Square))
            k.wait("vector", ks_)
            k.opf("vector", lambda e: e.tensor_reduce(
                linv.ap().rearrange("p (a b) -> p a b", b=1),
                ksq.ap().rearrange("p (g d) -> p g d", d=DH), axis=AX.X,
                op=ALU.add))
            rk = k.op("vector", lambda e: e.tensor_scalar(
                linv[:, :], linv[:, :], 1e-24, None, ALU.max))
            k.wait("gpsimd", tbk)
            if prev_t is not None:
                k.wait("gpsimd", prev_t)
            k.wait("scalar", rk)
            k.op("scalar", lambda e: e.activation(linv[:, :], linv[:, :], AF.Ln))
            rke = k.op("scalar", lambda e: e.activation(linv[:, :], linv[:, :],
                                                        AF.Exp, scale=-0.5))
            k.wait("gpsimd", rke)
            k.op("gpsimd", lambda e: e.tensor_tensor(
                v3(k_tok), v3(k_tok),
                linv.ap().unsqueeze(2).broadcast_to([128, HEADS, DH]),
                ALU.mult))
            k.op("gpsimd", lambda e: e.tensor_tensor(
                v3(kp2), v3(k_tok), hb(kcs), ALU.mult))
            k.op("gpsimd", lambda e: e.tensor_tensor(
                v4(krot)[:, :, 0:1, :], v4(k_tok)[:, :, 1:2, :],
                hb2(kss, 0), ALU.mult))
            k.op("gpsimd", lambda e: e.tensor_tensor(
                v4(krot)[:, :, 1:2, :], v4(k_tok)[:, :, 0:1, :],
                hb2(kss, 1), ALU.mult))
            kdone = k.op("gpsimd", lambda e: e.tensor_tensor(
                krot[:, :], krot[:, :], kp2[:, :], ALU.add))
            if d_att is not None and L == 0 and tau == 2:
                k.wait("sync", qdone)
                k.wait("sync", kdone)
                k.dma("sync", "dbgd", (lambda e, sl=sl: e.dma_start(
                    d_att[384:512, 192:704],
                    xsh[sl].ap().rearrange("p a b -> p (a b)").bitcast(F32))))
                k.dma("sync", "dbgd", (lambda e: e.dma_start(
                    d_att[640:768, 512:768],
                    wqkv_v[:, 0, 0:512].bitcast(F32))))
                k.dma("sync", "dbgd", (lambda e: e.dma_start(
                    d_att[256:384, 64:128], qcs[:, :])))
                k.dma("sync", "dbgd", (lambda e: e.dma_start(
                    d_att[256:384, 128:192], qss[:, :])))
                k.dma("sync", "dbgd", (lambda e: e.dma_start(
                    d_att[256:384, 192:256], qs_rep[:, :])))
                k.dma("sync", "dbgd", (lambda e: e.dma_start(
                    d_att[256:384, 256:320], tab4[:, 0, :])))
                k.dma("sync", "dbgd", (lambda e: e.dma_start(
                    d_att[512:640, 16:528],
                    qp2.ap().bitcast(F32)[:, 0:512])))
                k.dma("sync", "dbgd", (lambda e: e.dma_start(
                    d_att[0:128, 0:512], qrot.ap().bitcast(F32))))
                k.dma("sync", "dbgd", (lambda e: e.dma_start(
                    d_att[128:256, 0:512], krot.ap().bitcast(F32))))
                dsm = k.dma("sync", "dbgd", (lambda e: e.dma_start(
                    d_att[256:384, 0:16], rsq8[:, :])))
                k.wait("vector", dsm)
            k.wait("tensor", qdone)
            k.wait("tensor", kdone)
            qtm = None
            for kk in range(NKC):
                qtm = k.op("tensor", lambda e, kk=kk: e.matmul(
                    ps0[:, 128 * kk:128 * (kk + 1)],
                    qrot[:, 128 * kk:128 * (kk + 1)], id_b[:, :],
                    start=True, stop=True))
            ktm = None
            for kk in range(NKC):
                ktm = k.op("tensor", lambda e, kk=kk: e.matmul(
                    ps1[:, 128 * kk:128 * (kk + 1)],
                    krot[:, 128 * kk:128 * (kk + 1)], id_b[:, :],
                    start=True, stop=True))
            k.wait("scalar", qtm)
            if prev_t is not None:
                k.wait("scalar", prev_t)
            qev = k.op("scalar", lambda e: e.copy(
                q_fT.ap().rearrange("p a b -> p (a b)"), ps0[:, 0:1024]))
            k.wait("scalar", ktm)
            kev = k.op("scalar", lambda e, rs=rs: e.copy(
                kring[rs].ap().rearrange("p a b -> p (a b)"), ps1[:, 0:1024]))
            k.wait("tensor", kev)
            tiles = []
            for dtk in (2, 1, 0):
                tkt = tau - dtk
                if tkt >= 0:
                    mask = "lo" if dtk == 2 else ("up" if dtk == 0 else None)
                    tiles.append((tkt, mask))
            Wd = 128 * len(tiles)
            ex_tok = {}
            ptm_tok = {}
            ev_tok = {}
            pv_tok = {}
            ntl = len(tiles)

            def do_pt(hd):
                hp2 = hd % 2
                s4 = hd % 4
                k.wait("tensor", ex_tok[hd])
                if hd >= 4:
                    k.wait("tensor", ev_tok[hd - 4])
                for ti in range(ntl):
                    k.op("tensor", lambda e, hp2=hp2, s4=s4, ti=ti: e.matmul(
                        ps0[:, 512 * s4 + 128 * ti:512 * s4 + 128 * (ti + 1)],
                        P_sb[hp2][:, 128 * ti:128 * (ti + 1)], id_b[:, :],
                        start=True, stop=True))
                ptm_tok[hd] = k.now("tensor")
                k.wait("scalar", ptm_tok[hd])
                if hd >= 2:
                    k.wait("scalar", pv_tok[hd - 2])
                ev_tok[hd] = k.op("scalar", lambda e, hp2=hp2, s4=s4: e.copy(
                    PT_sb[hp2].ap().rearrange("p a b -> p (a b)")[:, 0:128 * ntl],
                    ps0[:, 512 * s4:512 * s4 + 128 * ntl]))

            def do_pv(hd):
                hp2 = hd % 2
                k.wait("tensor", ev_tok[hd])
                for ti, (tkt, mask) in enumerate(tiles):
                    vrs = tkt % 3
                    startmm = (ti == 0 and (hd % 8) == 0)
                    lastmm = (ti == ntl - 1 and (hd % 8) == 7)
                    k.op("tensor", lambda e, hp2=hp2, ti=ti, hd=hd, vrs=vrs,
                         s=startmm, l=lastmm: e.matmul(
                        ps1[:, 64 * hd:64 * (hd + 1)], PT_sb[hp2][:, ti, :],
                        vring[vrs][:, 64 * hd:64 * (hd + 1)],
                        start=s, stop=l))
                pv_tok[hd] = k.now("tensor")

            for hd in range(HEADS):
                sps = ps2 if (hd % 2 == 0) else ps3
                hp = hd % 2
                if hd >= 2:
                    k.wait("tensor", ex_tok[hd - 2])
                first = True
                k.op("tensor", lambda e, sps=sps, Wd=Wd: e.matmul(
                    sps[:, 0:Wd], id_b[:, :], mk3_b[:, 384 - Wd:384],
                    start=True, stop=False))
                first = False
                for ti, (tkt, mask) in enumerate(tiles):
                    last = (ti == len(tiles) - 1)
                    krs = tkt % 3
                    po = 64 * (hd % 2)
                    k.op("tensor", lambda e, sps=sps, ti=ti, hd=hd, krs=krs,
                         f=first, l=last, po=po: e.matmul(
                        sps[:, 128 * ti:128 * (ti + 1)],
                        q_fT.ap()[po:po + 64, hd // 2, :],
                        kring[krs].ap()[po:po + 64, hd // 2, :],
                        start=f, stop=l))
                    first = False
                sc_tok = k.now("tensor")
                if hd >= 1:
                    do_pt(hd - 1)
                if hd >= 2:
                    do_pv(hd - 2)
                k.wait("vector", sc_tok)
                bv = k.op("vector", lambda e, sps=sps, hd=hd, Wd=Wd: e.reduce_max(
                    nmax[:, hd:hd + 1], sps[:, 0:Wd], axis=AX.X, negate=True))
                k.wait("scalar", bv)
                if hd >= 2:
                    k.wait("scalar", ptm_tok[hd - 2])
                ex_tok[hd] = k.op("scalar", lambda e, sps=sps, hd=hd, hp=hp,
                                  Wd=Wd: e.activation(
                    P_sb[hp][:, 0:Wd], sps[:, 0:Wd], AF.Exp,
                    bias=nmax[:, hd:hd + 1], scale=1.0,
                    accum_out=lsum[:, hd:hd + 1]))
                if d_att is not None and L == 0 and tau == 2 and hd == 0:
                    k.wait("sync", ex_tok[hd])
                    k.dma("sync", "dbgd", (lambda e: e.dma_start(
                        d_att[384:512, 0:192],
                        P_sb[0].ap()[:, 0:384].bitcast(F32)[:, 0:192])))
                    k.dma("sync", "dbgd", (lambda e: e.dma_start(
                        d_att[512:640, 0:16], lsum[:, :])))
            do_pt(HEADS - 1)
            do_pv(HEADS - 2)
            do_pv(HEADS - 1)
            prev_s = ex_tok[HEADS - 1]
            prev_ev = ev_tok[HEADS - 1]
            k.wait("vector", pv_tok[HEADS - 1])
            k.wait("vector", prev_s)
            k.opf("vector", lambda e: e.reciprocal(linv[:, :], lsum[:, :]))
            cv1 = k.op("vector", lambda e: e.tensor_tensor(
                ctx_tok.ap().rearrange("p (g d) -> p g d", d=DH),
                ps1.ap().rearrange("p (g d) -> p g d", d=DH),
                linv.ap().unsqueeze(2).broadcast_to([128, HEADS, DH]),
                ALU.mult))
            if d_att is not None and L == 0 and tau == 2:
                k.wait("sync", cv1)
                k.dma("sync", "dbgd", (lambda e: e.dma_start(
                    d_att[640:768, 0:512],
                    ctx_tok.ap().bitcast(F32)[:, 0:512])))
                k.dma("sync", "dbgd", (lambda e: e.dma_start(
                    d_att[768:896, 0:16], linv[:, :])))
            k.wait("tensor", cv1)
            k.wait("tensor", prev_ev)
            ctm = None
            for kk in range(NKC):
                ctm = k.op("tensor", lambda e, kk=kk: e.matmul(
                    ps0[:, 128 * kk:128 * (kk + 1)],
                    ctx_tok[:, 128 * kk:128 * (kk + 1)], id_b[:, :],
                    start=True, stop=True))
            k.wait("scalar", ctm)
            cev = k.op("scalar", lambda e: e.copy(
                ctx_fT.ap().rearrange("p a b -> p (a b)"), ps0[:, 0:1024]))
            prev_ev = cev
            k.wait("tensor", cev)
            k.wait("tensor", wo_tok)
            wo_ev = {}
            for m in range(NKC):
                spo = ps2 if (m % 2 == 0) else ps3
                if m >= 2:
                    k.wait("tensor", wo_ev[m - 2])
                wot = None
                for kk in range(NKC):
                    wot = k.op("tensor", lambda e, m=m, kk=kk, spo=spo: e.matmul(
                        spo[:, 0:128], wo_v[:, kk, 128 * m:128 * (m + 1)],
                        ctx_fT[:, kk, :], start=(kk == 0),
                        stop=(kk == NKC - 1)))
                k.wait("vector", wot)
                wo_ev[m] = k.op("vector", lambda e, m=m, spo=spo, tau=tau:
                                e.tensor_tensor(
                    h[m][:, 1 + 128 * tau:1 + 128 * (tau + 1)], spo[:, 0:128],
                    h[m][:, 1 + 128 * tau:1 + 128 * (tau + 1)], ALU.add))
            prev_t = k.now("tensor")
            prev_v = k.now("vector")
        dd = dbg(f"hA{L}", (NKC * 128, T))
        if dd is not None:
            tok = k.now("vector")
            k.wait("sync", tok)
            for i in range(NKC):
                k.dma("sync", "dbgd", (lambda e, i=i, dd=dd: e.dma_start(
                    dd[128 * i:128 * (i + 1), :], h[i][:, 1:1 + T])))
        for _e in Bld.ENG:
            if "dbgd" in k.dsem:
                k.wait(_e, k.dtok("dbgd"))
        k.barrier()

        for c2 in range(NC2):
            w = 128 if c2 < 21 else 42
            k.dma("gpsimd", "w2l", (lambda e, c2=c2, w=w, L=L: e.dma_start(
                w2_v[0:w, c2, :], W2_d[L, 128 * c2:128 * c2 + w, :])))
        w2_tok = k.dtok("w2l")
        w1_guard = {}
        gg_guard = {}
        w1idx = 0
        for half in range(2):
            tb0 = 1024 * half
            for q4 in range(4):
                t0 = tb0 + 256 * q4
                smm = None
                for kk in range(NKC):
                    cpa = k.op("vector", lambda e, kk=kk, t0=t0: e.tensor_copy(
                        xh_b[:, :], hsl(kk, t0, 256, True)))
                    k.wait("scalar", cpa)
                    cps = k.op("scalar", lambda e: e.activation(
                        sq_b[:, :], xh_b[:, :], AF.Square))
                    k.wait("tensor", cps)
                    k.op("tensor", lambda e, kk=kk: e.matmul(
                        ps2[:, 0:256], ones128_b[:, :], xh_b[:, :],
                        start=(kk == 0), stop=(kk == NKC - 1)))
                    smm = k.op("tensor", lambda e, kk=kk: e.matmul(
                        ps3[:, 0:256], ones128_b[:, :], sq_b[:, :],
                        start=(kk == 0), stop=(kk == NKC - 1)))
                    k.wait("vector", smm)
                    k.wait("scalar", smm)
                k.op("vector", lambda e: e.tensor_scalar(
                    stat_t[:, :], ps2[:, 0:256], 1.0 / D, None, ALU.mult))
                k.op("vector", lambda e: e.tensor_tensor(
                    stat_c[:, :], stat_t[:, :], stat_t[:, :], ALU.mult))
                va = k.op("vector", lambda e: e.scalar_tensor_tensor(
                    stat_a[:, :], ps3[:, 0:256], 1.0 / D, stat_c[:, :],
                    ALU.mult, ALU.subtract))
                k.wait("scalar", va)
                k.op("scalar", lambda e: e.activation(
                    stat_a[:, :], stat_a[:, :], AF.Ln, bias=eps_col[:, :]))
                sa = k.op("scalar", lambda e: e.activation(
                    stat_a[:, :], stat_a[:, :], AF.Exp, scale=-0.5))
                k.wait("vector", sa)
                cc_ = k.op("vector", lambda e: e.scalar_tensor_tensor(
                    stat_c[:, :], stat_t[:, :], -1.0, stat_a[:, :],
                    ALU.mult, ALU.mult))
                k.wait("tensor", cc_)
                for kk in range(NKC):
                    k.op("vector", lambda e, kk=kk, t0=t0: e.tensor_tensor(
                        stat_t[:, :], hsl(kk, t0, 256, True), stat_a[:, :],
                        ALU.mult))
                    k.op("vector", lambda e: e.tensor_add(
                        stat_t[:, :], stat_t[:, :], stat_c[:, :]))
                    k.op("vector", lambda e, kk=kk, t0=t0, tb0=tb0:
                         e.tensor_scalar(
                        xn2_v[:, kk, t0 - tb0:t0 - tb0 + 256], stat_t[:, :],
                        lng_c[:, kk:kk + 1], lnb_c[:, kk:kk + 1],
                        ALU.mult, ALU.add))
            xn2_tok = k.now("vector")
            k.wait("tensor", xn2_tok)
            k.wait("gpsimd", xn2_tok)
            for c2 in range(NC2):
                w = 128 if c2 < 21 else 42
                pp = c2 % 2
                if w1idx - 1 in w1_guard:
                    k.wait("gpsimd", w1_guard[w1idx - 1])
                k.dma("gpsimd", "w1l", (lambda e, c2=c2, w=w, L=L, pp=pp:
                                        e.dma_start(
                    w1a_s[pp].ap()[:, :, 0:w],
                    W1_d[L, :, 128 * c2:128 * c2 + w].rearrange(
                        "(kk p) n -> p kk n", p=128))))
                lg = k.dma("gpsimd", "w1l", (lambda e, c2=c2, w=w, L=L, pp=pp:
                                             e.dma_start(
                    w1g_s[pp].ap()[:, :, 0:w],
                    W1_d[L, :, DI + 128 * c2:DI + 128 * c2 + w].rearrange(
                        "(kk p) n -> p kk n", p=128))))
                k.wait("tensor", lg)
                if w1idx - 1 in gg_guard:
                    k.wait("tensor", gg_guard[w1idx - 1][0])
                    k.wait("tensor", gg_guard[w1idx - 1][1])
                wlast = None
                for sp in range(2):
                    for kk in range(NKC):
                        k.op("tensor", lambda e, kk=kk, sp=sp, w=w, pp=pp:
                             e.matmul(
                            ps0[0:w, 512 * sp:512 * (sp + 1)],
                            w1a_s[pp].ap()[:, kk, 0:w],
                            xn2_v[:, kk, 512 * sp:512 * (sp + 1)],
                            start=(kk == 0), stop=(kk == NKC - 1)))
                        wlast = k.op("tensor", lambda e, kk=kk, sp=sp, w=w,
                                     pp=pp: e.matmul(
                            ps0[0:w, 1024 + 512 * sp:1024 + 512 * (sp + 1)],
                            w1g_s[pp].ap()[:, kk, 0:w],
                            xn2_v[:, kk, 512 * sp:512 * (sp + 1)],
                            start=(kk == 0), stop=(kk == NKC - 1)))
                w1_guard[w1idx] = wlast
                k.wait("scalar", wlast)
                ge = k.op("scalar", lambda e, w=w, c2=c2: e.activation(
                    gelu_t[0:w, :], ps0[0:w, 1024:2048], AF.Gelu,
                    bias=b1_c[0:w, 22 + c2:23 + c2]))
                k.wait("vector", ge)
                pp2 = w1idx % 2
                if w1idx - 2 in gg_guard:
                    pass
                gv = k.op("vector", lambda e, w=w, c2=c2, pp2=pp2:
                          e.scalar_tensor_tensor(
                    ggs[pp2][0:w, :], ps0[0:w, 0:1024], b1_c[0:w, c2:c2 + 1],
                    gelu_t[0:w, :], ALU.add, ALU.mult))
                gg_guard[w1idx] = (gv, ge)
                k.wait("sync", gv)
                gst = k.dma("sync", "ggst", (lambda e, c2=c2, w=w, half=half,
                                             pp2=pp2: e.dma_start(
                    gg_scr[c2, 0:w, 1024 * half:1024 * (half + 1)],
                    ggs[pp2][0:w, :])))
                k.wait("vector", gst)
                w1idx += 1
        gg_tok = k.dtok("ggst")
        k.wait("tensor", w2_tok)
        k.wait("gpsimd", gg_tok)
        lastg = k.now("tensor")
        k.wait("tensor", gg_guard[w1idx - 1][0])
        k.wait("tensor", gg_guard[w1idx - 2][0])
        prev_tile_mm = None
        for q8 in range(8):
            t0 = 256 * q8
            if prev_tile_mm is not None:
                k.wait("gpsimd", prev_tile_mm)
            gg_ld = {}
            for c2 in range(NC2):
                w = 128 if c2 < 21 else 42
                pp = c2 % 2
                if c2 - 1 in gg_ld:
                    k.wait("gpsimd", gg_ld[c2 - 1])
                gl = k.dma("gpsimd", "ggld", (lambda e, c2=c2, w=w, t0=t0, pp=pp:
                                              e.dma_start(
                    gg_in[pp][0:w, :], gg_scr[c2, 0:w, t0:t0 + 256])))
                k.wait("tensor", gl)
                for m in range(NKC):
                    k.op("tensor", lambda e, c2=c2, m=m, w=w, pp=pp: e.matmul(
                        ps0[:, 256 * m:256 * (m + 1)],
                        w2_v[0:w, c2, 128 * m:128 * (m + 1)],
                        gg_in[pp][0:w, :],
                        start=(c2 == 0 and (m % 2) == 0),
                        stop=(c2 == NC2 - 1 and (m % 2) == 1)))
                gg_ld[c2] = k.now("tensor")
            w2_done = k.now("tensor")
            prev_tile_mm = w2_done
            k.wait("vector", w2_done)
            ev = None
            for m in range(NKC):
                ev = k.op("vector", lambda e, m=m, t0=t0: e.scalar_tensor_tensor(
                    h[m][:, 1 + t0:1 + t0 + 256], ps0[:, 256 * m:256 * (m + 1)],
                    b2_c[:, m:m + 1], h[m][:, 1 + t0:1 + t0 + 256],
                    ALU.add, ALU.add))
            k.wait("tensor", ev)
        dd = dbg(f"hF{L}", (NKC * 128, T))
        if dd is not None:
            tok = k.now("vector")
            k.wait("sync", tok)
            for i in range(NKC):
                k.dma("sync", "dbgd", (lambda e, i=i, dd=dd: e.dma_start(
                    dd[128 * i:128 * (i + 1), :], h[i][:, 1:1 + T])))
        for _e in Bld.ENG:
            if "dbgd" in k.dsem:
                k.wait(_e, k.dtok("dbgd"))
        k.barrier()

    out_tok = None
    fin = None
    for tau in range(NT):
        if fin is not None:
            k.wait("gpsimd", fin)
        k.wait("gpsimd", k.dtok("yst"))
        lx = k.dma("gpsimd", "fin", (lambda e, tau=tau: e.dma_start(
            io_a[:, :], x_d[128 * tau:128 * (tau + 1), :])))
        ly = k.dma("gpsimd", "fin", (lambda e, tau=tau: e.dma_start(
            io_b[:, :], y_scr[tau, :, :])))
        tt = None
        for kk in range(NKC):
            tt = k.op("tensor", lambda e, kk=kk, tau=tau: e.transpose(
                ps0[:, 128 * kk:128 * (kk + 1)],
                h[kk][:, 1 + 128 * tau:1 + 128 * (tau + 1)], id_f[:, :]))
        k.wait("vector", tt)
        k.wait("vector", lx)
        k.wait("vector", ly)
        k.op("vector", lambda e: e.tensor_add(io_a[:, :], io_a[:, :],
                                              io_b[:, :]))
        fin = k.op("vector", lambda e: e.tensor_add(io_c[:, :], ps0[:, 0:1024],
                                                    io_a[:, :]))
        k.wait("tensor", fin)
        k.wait("sync", fin)
        out_tok = k.dma("sync", "outst", (lambda e, tau=tau: e.dma_start(
            out_d[128 * tau:128 * (tau + 1), :], io_c[:, :])))
    k.wait("sync", out_tok)
    k.wait("vector", out_tok)

    k.finalize()
    nc._kernel_es = es
    return nc, dbgt


def kernel(**inputs):
    x_all = np.ascontiguousarray(inputs["L0_attn_sequence"])
    n_cores = x_all.shape[0]
    HCv = host_consts()

    nc, _ = build_program()

    base = {
        "A": np.ascontiguousarray(inputs["L0_attn_A"], dtype=np.float32),
        "W_B": np.ascontiguousarray(inputs["L0_attn_W_B"], dtype=np.float32),
        "b_B": np.ascontiguousarray(inputs["L0_attn_b_B"], dtype=np.float32),
        "W_C": np.ascontiguousarray(inputs["L0_attn_W_C"], dtype=np.float32),
        "b_C": np.ascontiguousarray(inputs["L0_attn_b_C"], dtype=np.float32),
        "W_Delta": np.ascontiguousarray(inputs["L0_attn_W_Delta"], dtype=np.float32),
        "b_Delta": np.ascontiguousarray(inputs["L0_attn_b_Delta"], dtype=np.float32),
        "bias_Delta": np.ascontiguousarray(inputs["L0_attn_bias_Delta"], dtype=np.float32),
        "Wqkv": np.ascontiguousarray(inputs["L0_ffn_Wqkv"], dtype=np.float32),
        "q_scale": np.ascontiguousarray(inputs["L0_ffn_q_scale"], dtype=np.float32),
        "k_scale": np.ascontiguousarray(inputs["L0_ffn_k_scale"], dtype=np.float32),
        "Wo": np.ascontiguousarray(inputs["L0_ffn_Wo"], dtype=np.float32),
        "ln_g": np.ascontiguousarray(inputs["L0_ffn_ln_g"], dtype=np.float32),
        "ln_b": np.ascontiguousarray(inputs["L0_ffn_ln_b"], dtype=np.float32),
        "W1": np.ascontiguousarray(inputs["L0_ffn_W1"], dtype=np.float32),
        "b1": np.ascontiguousarray(inputs["L0_ffn_b1"], dtype=np.float32),
        "W2": np.ascontiguousarray(inputs["L0_ffn_W2"], dtype=np.float32),
        "b2": np.ascontiguousarray(inputs["L0_ffn_b2"], dtype=np.float32),
    }
    for k2, v in HCv.items():
        base["c_" + k2] = v

    in_maps = []
    for c in range(n_cores):
        m = dict(base)
        m["x"] = np.ascontiguousarray(x_all[c], dtype=np.float32)
        in_maps.append(m)

    res = run_bass_kernel_spmd(nc, in_maps, core_ids=list(range(n_cores)))
    out = np.stack([np.asarray(r["out"], dtype=np.float32) for r in res.results])
    return out


# version 15
# speedup vs baseline: 1.0229x; 1.0229x over previous
import math
import numpy as np
import ml_dtypes
import concourse.bass as bass
import concourse.mybir as mybir
from concourse.bass_utils import run_bass_kernel_spmd
from contextlib import ExitStack

F32 = mybir.dt.float32
BF16 = mybir.dt.bfloat16
I32 = mybir.dt.int32
AF = mybir.ActivationFunctionType
ALU = mybir.AluOpType
AX = mybir.AxisListType

T = 2048
D = 1024
NKC = 8
NT = 16
NST = 16
HEADS = 16
DH = 64
DI = 2730
W1C = 5460
NC2 = 22
DEPTH6 = 6
NEGBIG = -30000.0
PI = math.pi
TWO_PI = 2.0 * math.pi
C1 = float(np.float32(TWO_PI))
C2 = float(TWO_PI - C1)
MC = 256
NMC = T // MC
LN_EPS = 1e-5


_TABS_CACHE = [None]


def _jax_tables():
    if _TABS_CACHE[0] is not None:
        return _TABS_CACHE[0]
    import jax
    import jax.numpy as jnp
    with jax.default_device(jax.devices("cpu")[0]):
        half = jnp.arange(0, DH, 2, dtype=jnp.float32)
        inv_freq = 1.0 / 10000.0 ** (half / DH)
        t = jnp.arange(T, dtype=jnp.float32)
        freqs = t[:, None] * inv_freq[None, :]
        freqs = jnp.concatenate([freqs, freqs], -1)
        base_scale = (half + 0.4 * DH) / (1.4 * DH)
        power = (t - T // 2) / (256 // 2)
        scale = base_scale[None, :] ** power[:, None]
        scale = jnp.concatenate([scale, scale], -1)
        cos = np.asarray(jnp.cos(freqs), dtype=np.float32)
        sin = np.asarray(jnp.sin(freqs), dtype=np.float32)
        scl = np.asarray(scale, dtype=np.float32)
        inv = np.asarray(1.0 / scale, dtype=np.float32)
    sinS = np.concatenate([-sin[:, :32], sin[:, 32:]], axis=1)
    _TABS_CACHE[0] = np.ascontiguousarray(
        np.stack([cos, sinS, scl, inv]), dtype=np.float32)
    return _TABS_CACHE[0]


def host_consts():
    ii = np.arange(128)
    c = {}
    c["tabs"] = _jax_tables()
    c["idm"] = np.eye(128, dtype=np.float32)
    c["mku"] = np.where(ii[None, :] < ii[:, None], NEGBIG, 0.0).astype(np.float32)
    c["mkl"] = np.where(ii[None, :] > ii[:, None], NEGBIG, 0.0).astype(np.float32)
    c["mk3"] = np.concatenate(
        [np.where(ii[None, :] < ii[:, None], NEGBIG, 0.0),
         np.zeros((128, 128)),
         np.where(ii[None, :] > ii[:, None], NEGBIG, 0.0)],
        axis=1).astype(np.float32)
    c["ones1"] = np.ones((1, 128), dtype=np.float32)
    c["ones128"] = np.ones((128, 128), dtype=np.float32)
    c["p_col"] = ii.astype(np.float32).reshape(128, 1)
    c["pquot"] = (ii.astype(np.float64) / 128.0).astype(np.float32).reshape(128, 1)
    c["half_row"] = np.arange(0, DH, 2, dtype=np.float32).reshape(1, DH // 2)
    return c


class Bld:
    ENG = ("tensor", "vector", "scalar", "gpsimd", "sync")

    def __init__(self, nc, es):
        self.nc = nc
        self.es = es
        self.q = {e: [] for e in self.ENG}
        self.prog = {}
        self.prog_n = {}
        self.last_wait = {e: {} for e in self.ENG}
        self.nsem = 0
        for e in self.ENG:
            self._new_prog(e)
        self.dsem = {}
        self.dn = {}
        self.dcarry = {}
        self.bar_n = 0
        self.bar_sem = self._sem("bars")

    def _sem(self, name):
        self.nsem += 1
        return self.es.enter_context(self.nc.semaphore(f"{name}{self.nsem}"))

    def _new_prog(self, e):
        self.prog[e] = self._sem(f"p_{e}_")
        self.prog_n[e] = 0

    def op(self, e, fn):
        if self.prog_n[e] >= 28000:
            self._new_prog(e)
        h = self.prog[e]
        self.prog_n[e] += 1
        v = self.prog_n[e]

        def go(eng, fn=fn, h=h):
            fn(eng).then_inc(h, 1)
        self.q[e].append(go)
        return (h, v)

    def now(self, e):
        return (self.prog[e], self.prog_n[e])

    def opf(self, e, fn):
        tok = self.op(e, fn)
        self.wait(e, tok)
        return tok

    def wait(self, e, tok):
        if tok is None:
            return
        if isinstance(tok, list):
            for t in tok:
                self.wait(e, t)
            return
        h, v = tok
        lw = self.last_wait[e]
        if lw.get(id(h), -1) >= v:
            return
        lw[id(h)] = v

        def go(eng, h=h, v=v):
            eng.wait_ge(h, v)
        self.q[e].append(go)

    def dma(self, e, name, fn):
        if name not in self.dsem:
            self.dsem[name] = self._sem(f"d_{name}")
            self.dn[name] = 0
            self.dcarry[name] = []
        elif self.dn[name] >= 29000:
            self.dcarry[name] = self.dcarry.get(name, []) + [
                (self.dsem[name], self.dn[name])]
            self.dsem[name] = self._sem(f"d_{name}")
            self.dn[name] = 0
        h = self.dsem[name]
        self.dn[name] += 16
        v = self.dn[name]

        def go(eng, fn=fn, h=h):
            fn(eng).then_inc(h, 16)
        self.q[e].append(go)
        return [(h, v)] + self.dcarry.get(name, [])

    def dtok(self, name):
        return [(self.dsem[name], self.dn[name])] + self.dcarry.get(name, [])

    def barrier(self):
        bar = self.bar_sem
        n = len(self.ENG)
        self.bar_n += n
        tgt = self.bar_n
        for e in self.ENG:
            def go(eng, bar=bar):
                eng.nop().then_inc(bar, 1)
            self.q[e].append(go)
        for e in self.ENG:
            def go2(eng, bar=bar, v=tgt):
                eng.wait_ge(bar, v)
            self.q[e].append(go2)
        if self.bar_n > 28000:
            self.bar_sem = self._sem("bars")
            self.bar_n = 0

    def finalize(self):
        with self.nc.Block() as block:
            for e in self.ENG:
                qe = self.q[e]

                def runner(eng, qe=qe):
                    for fn in qe:
                        fn(eng)
                getattr(block, e)(runner)


def build_program(debug=()):
    nc = bass.Bass("TRN2", target_bir_lowering=False,
                   detect_race_conditions=False)
    es = ExitStack()
    dbg_set = set(debug)
    dbgt = {}

    def din(name, shape):
        return nc.dram_tensor(name, list(shape), F32, kind="ExternalInput").ap()

    x_d = din("x", (T, D))
    A_d = din("A", (D, NST))
    W_B_d = din("W_B", (NST, D))
    b_B_d = din("b_B", (NST,))
    W_C_d = din("W_C", (NST, D))
    b_C_d = din("b_C", (NST,))
    W_Dl_d = din("W_Delta", (1, D))
    b_Dl_d = din("b_Delta", (1,))
    bias_Dl_d = din("bias_Delta", (1,))
    Wqkv_d = din("Wqkv", (DEPTH6, D, 3 * D))
    qs_d = din("q_scale", (DEPTH6, DH))
    ks_d = din("k_scale", (DEPTH6, DH))
    Wo_d = din("Wo", (DEPTH6, D, D))
    lng_d = din("ln_g", (DEPTH6, D))
    lnb_d = din("ln_b", (DEPTH6, D))
    W1_d = din("W1", (DEPTH6, D, W1C))
    b1_d = din("b1", (DEPTH6, W1C))
    W2_d = din("W2", (DEPTH6, DI, D))
    b2_d = din("b2", (DEPTH6, D))
    HCv = host_consts()
    HC = {k2: din("c_" + k2, v.shape) for k2, v in HCv.items()}

    out_d = nc.dram_tensor("out", [T, D], F32, kind="ExternalOutput").ap()
    xn_scr = nc.dram_tensor("xn_scr", [NKC, 128, T], BF16, kind="Internal").ap()
    y_scr = nc.dram_tensor("y_scr", [NT, 128, D], F32, kind="Internal").ap()
    gg_scr = nc.dram_tensor("gg_scr", [NC2, 128, T], BF16, kind="Internal").ap()
    bmd_scr = nc.dram_tensor("bmd_scr", [17, T], F32, kind="Internal").ap()

    def dbg(name, shape):
        if name in dbg_set:
            dbgt[name] = nc.dram_tensor("dbg_" + name, list(shape), F32,
                                        kind="ExternalOutput").ap()
            return dbgt[name]
        return None

    k = Bld(nc, es)
    sb = lambda name, shape, dt: es.enter_context(nc.sbuf_tensor(name, shape, dt))

    h = [sb(f"h{i}", [128, 2050], F32) for i in range(NKC)]
    regAb = sb("regAb", [128, 24576], BF16)
    wqkv_v = regAb.ap()[:, 0:24576].rearrange("p (kk n) -> p kk n", kk=NKC)
    w2_v = regAb.ap()[:, 0:22528].rearrange("p (c n) -> p c n", c=NC2)
    mv_xn = regAb.ap()[:, 0:2048].rearrange("p (kk t) -> p kk t", kk=NKC)
    mv_h = regAb.ap()[:, 2048:4096].rearrange("p (kk t) -> p kk t", kk=NKC)
    mv_hT = regAb.ap()[:, 4096:6144].rearrange("p (j d) -> p j d", j=MC // 128)
    regBb = sb("regBb", [128, 8192], BF16)
    wo_v = regBb.ap()[:, 0:8192].rearrange("p (kk n) -> p kk n", kk=NKC)
    xn2_v = regBb.ap()[:, 0:8192].rearrange("p (kk t) -> p kk t", kk=NKC)
    scrF = sb("scrF", [128, 5376], F32)
    io_a = scrF.ap()[:, 0:1024]
    io_b = scrF.ap()[:, 1024:2048]
    io_c = scrF.ap()[:, 2048:3072]
    mv_bmdf = scrF.ap()[:, 0:2048]
    mv_db = scrF.ap()[:, 2048:2304]
    mv_bm = scrF.ap()[:, 2304:2560]
    mv_ab = [scrF.ap()[:, 2560:2816], scrF.ap()[:, 0:256]]
    mv_wt = [scrF.ap()[:, 2816:3072], scrF.ap()[:, 256:512]]
    mv_bx = [scrF.ap()[:, 3072:3328], scrF.ap()[:, 512:768]]
    mv_xb = [scrF.ap()[:, 768:1024], scrF.ap()[:, 1024:1280]]
    mv_y = scrF.ap()[:, 3328:5376].rearrange("p (j d) -> p j d", j=MC // 128)

    id_f = sb("id_f", [128, 128], F32)
    id_b = sb("id_b", [128, 128], BF16)
    mk3_b = sb("mk3", [128, 384], BF16)
    ones1_f = sb("ones1_f", [1, 128], F32)
    ones1_b = sb("ones1_b", [1, 128], BF16)
    ones128_b = sb("ones128_b", [128, 128], BF16)
    p_col = sb("p_col", [128, 1], F32)
    pquot = sb("pquot", [128, 1], F32)
    half_row = sb("half_row", [1, DH // 2], F32)
    eps_col = sb("eps_col", [128, 1], F32)
    one_col = sb("one_col", [128, 1], F32)
    tcmb = sb("tcmb", [128, DH], F32)
    tab4 = sb("tab4", [128, 4, DH], F32)
    qcs = sb("qcs", [128, DH], F32)
    qss = sb("qss", [128, DH], F32)
    kcs = sb("kcs", [128, DH], F32)
    kss = sb("kss", [128, DH], F32)
    qs_rep = sb("qs_rep", [128, DH], F32)
    ks_rep = sb("ks_rep", [128, DH], F32)
    qs_swp = sb("qs_swp", [128, DH], F32)
    ks_swp = sb("ks_swp", [128, DH], F32)
    lng_c = sb("lng_c", [128, NKC], F32)
    lnb_c = sb("lnb_c", [128, NKC], F32)
    b1_c = sb("b1_c", [128, 44], F32)
    b2_c = sb("b2_c", [128, NKC], F32)
    small_st = sb("small_st", [48, 128], F32)
    qsrow = sb("qsrow", [1, DH], F32)
    ksrow = sb("ksrow", [1, DH], F32)
    A_f = sb("A_f", [128, NKC, NST], F32)
    invA_f = sb("invA_f", [128, NKC, NST], F32)
    WBCD = sb("WBCD", [128, NKC, 33], BF16)
    xnb_st = sb("xnb_st", [128, 128], BF16)
    brow_f = sb("brow_f", [1, 34], F32)
    brow_b = sb("brow_b", [1, 33], BF16)
    BCD = sb("BCD", [128, NT, 33], F32)
    st_cols = sb("st_cols", [128, NST, NKC], F32)
    sm1 = sb("sm1", [128, NT], F32)
    sm2 = sb("sm2", [128, NT], F32)
    xsh = [sb(f"xsh{i}", [128, NKC, 128], BF16) for i in range(2)]
    hbnd = scrF.ap()[:, 3328:3392].rearrange("p (a b) -> p a b", a=4)
    q_tok = sb("q_tok", [128, 1024], F32)
    k_tok = sb("k_tok", [128, 1024], F32)
    qsq = sb("qsq", [128, 1024], BF16)
    ksq = sb("ksq", [128, 1024], BF16)
    qp2 = qsq
    kp2 = ksq
    qrot = sb("qrot", [128, 1024], BF16)
    krot = sb("krot", [128, 1024], BF16)
    q_fT = sb("q_fT", [128, NKC, 128], BF16)
    kring = [sb(f"kring{i}", [128, NKC, 128], BF16) for i in range(3)]
    vring = [sb(f"vring{i}", [128, 1024], BF16) for i in range(3)]
    rsq8 = sb("rsq8", [128, HEADS], F32)
    nmax = sb("nmax", [128, HEADS], F32)
    bias_h = sb("bias_h", [128, HEADS], F32)
    lsum = sb("lsum", [128, HEADS], F32)
    linv = sb("linv", [128, HEADS], F32)
    ssc = sb("ssc", [128, HEADS], F32)
    P_sb = [sb(f"P{i}", [128, 384], BF16) for i in range(2)]
    PT_sb = [sb(f"PT{i}", [128, 3, 128], BF16) for i in range(2)]
    ctx_tok = sb("ctx_tok", [128, 1024], BF16)
    ctx_fT = sb("ctx_fT", [128, NKC, 128], BF16)
    _w1a = sb("w1a", [128, NKC, 128], BF16)
    _w1g = sb("w1g", [128, NKC, 128], BF16)
    w1a_s = [_w1a, _w1a]
    w1g_s = [_w1g, _w1g]
    gelu_t = q_tok
    ggs = [ctx_tok, krot]
    gg_in = [qrot.ap()[:, 0:256], qrot.ap()[:, 256:512]]
    stat_a = k_tok.ap()[:, 0:256]
    stat_c = k_tok.ap()[:, 256:512]
    stat_t = k_tok.ap()[:, 512:768]
    sq_b = sb("sq_b", [128, 256], BF16)
    xh_b = sb("xh_b", [128, 256], BF16)
    ps0 = es.enter_context(nc.psum_tensor("ps0", [128, 2048], F32))
    ps1 = es.enter_context(nc.psum_tensor("ps1", [128, 1024], F32))
    ps2 = es.enter_context(nc.psum_tensor("ps2", [128, 512], F32))
    ps3 = es.enter_context(nc.psum_tensor("ps3", [128, 512], F32))

    def hsl(kk, t0, n, shifted):
        off = 1 + t0 - (1 if (shifted and kk >= 4) else 0)
        return h[kk][:, off:off + n]

    def hb(t):
        return t.ap().unsqueeze(1).broadcast_to([128, HEADS, DH])

    def hb2(t, half):
        v = t.ap().rearrange("p (two d) -> p two d", two=2)[:, half:half + 1, :]
        return v.unsqueeze(1).broadcast_to([128, HEADS, 1, 32])

    def v3(t):
        return t.ap().rearrange("p (g d) -> p g d", g=HEADS)

    def v4(t):
        return t.ap().rearrange("p (g two d) -> p g two d", g=HEADS, two=2)

    for t_sb, cname, eng in [
        (id_f, "idm", "sync"), (id_b, "idm", "gpsimd"),
        (mk3_b, "mk3", "gpsimd"),
        (ones1_f, "ones1", "sync"), (ones1_b, "ones1", "gpsimd"),
        (ones128_b, "ones128", "gpsimd"), (p_col, "p_col", "sync"),
        (pquot, "pquot", "sync"), (half_row, "half_row", "sync"),
    ]:
        k.dma(eng, "cst_" + eng, (lambda e, t=t_sb, dd=HC[cname]:
                                  e.dma_start(t[:, :], dd)))
    for e in Bld.ENG:
        k.wait(e, k.dtok("cst_sync"))
        k.wait(e, k.dtok("cst_gpsimd"))
    k.opf("vector", lambda e: e.memset(eps_col[:, :], 0.0))
    k.opf("vector", lambda e: e.tensor_scalar(eps_col[:, :], eps_col[:, :], LN_EPS,
                                              None, ALU.add))
    k.opf("vector", lambda e: e.memset(one_col[:, :], 0.0))
    k.opf("vector", lambda e: e.tensor_scalar(one_col[:, :], one_col[:, :], 1.0,
                                              None, ALU.add))
    for i in range(NKC):
        k.op("vector", lambda e, i=i: e.memset(h[i][:, 0:1], 0.0))
    k.op("vector", lambda e: e.memset(st_cols[:, :, :], 0.0))
    k.barrier()

    for tau in range(NT):
        ld = k.dma("gpsimd", "xin", (lambda e, tau=tau: e.dma_start(
            io_a[:, :], x_d[128 * tau:128 * (tau + 1), :])))
        k.wait("vector", ld)
        k.opf("vector", lambda e: e.reduce_sum(ssc[:, 0:1], io_a[:, :], axis=AX.X))
        k.opf("vector", lambda e: e.tensor_scalar(ssc[:, 1:2], ssc[:, 0:1],
                                                  1.0 / D, None, ALU.mult))
        k.op("vector", lambda e: e.tensor_scalar(io_b[:, :], io_a[:, :],
                                                 ssc[:, 1:2], None, ALU.subtract))
        sqd = k.op("vector", lambda e: e.tensor_mul(io_c[:, :], io_b[:, :],
                                                    io_b[:, :]))
        k.wait("gpsimd", sqd)
        vs = k.op("vector", lambda e: e.reduce_sum(ssc[:, 2:3], io_c[:, :],
                                                   axis=AX.X))
        k.wait("scalar", vs)
        k.op("scalar", lambda e: e.activation(ssc[:, 3:4], ssc[:, 2:3], AF.Ln,
                                              bias=eps_col[:, :], scale=1.0 / D))
        s2 = k.op("scalar", lambda e: e.activation(ssc[:, 4:5], ssc[:, 3:4],
                                                   AF.Exp, scale=-0.5))
        k.wait("vector", s2)
        xn_done = k.op("vector", lambda e: e.tensor_scalar(
            io_b[:, :], io_b[:, :], ssc[:, 4:5], None, ALU.mult))
        k.wait("tensor", xn_done)
        for half in range(2):
            tt = None
            for j in range(4):
                c = 4 * half + j
                tt = k.op("tensor", lambda e, c=c, j=j: e.transpose(
                    ps0[:, 512 * j:512 * j + 128], io_b[:, 128 * c:128 * (c + 1)],
                    id_f[:, :]))
            k.wait("vector", tt)
            ev = None
            for j in range(4):
                c = 4 * half + j
                ev = k.op("vector", lambda e, c=c, j=j, tau=tau: e.tensor_copy(
                    h[c][:, 1 + 128 * tau:1 + 128 * (tau + 1)],
                    ps0[:, 512 * j:512 * j + 128]))
            k.wait("tensor", ev)
    htok = k.now("vector")
    for i in range(NKC):
        k.wait("gpsimd", htok)
        k.dma("gpsimd", "xnst", (lambda e, i=i: e.dma_start(
            xn_scr[i, :, :], h[i][:, 1:1 + T])))
    d = dbg("xn", (NKC * 128, T))
    if d is not None:
        for i in range(NKC):
            k.wait("sync", htok)
            k.dma("sync", "dbgd", (lambda e, i=i, d=d: e.dma_start(
                d[128 * i:128 * (i + 1), :], h[i][:, 1:1 + T])))
    k.barrier()

    for dt in range(NKC):
        k.dma("gpsimd", "msm", (lambda e, dt=dt: e.dma_start(
            A_f[:, dt, :], A_d[128 * dt:128 * (dt + 1), :])))
    k.wait("vector", k.dtok("msm"))
    for dt in range(NKC):
        k.op("vector", lambda e, dt=dt: e.reciprocal(invA_f[:, dt, :],
                                                     A_f[:, dt, :]))
    for kk in range(NKC):
        k.dma("sync", "msm2", (lambda e, kk=kk: e.dma_start(
            small_st[0:16, :], W_B_d[:, 128 * kk:128 * (kk + 1)])))
        k.dma("sync", "msm2", (lambda e, kk=kk: e.dma_start(
            small_st[16:32, :], W_C_d[:, 128 * kk:128 * (kk + 1)])))
        l3 = k.dma("sync", "msm2", (lambda e, kk=kk: e.dma_start(
            small_st[32:33, :], W_Dl_d[:, 128 * kk:128 * (kk + 1)])))
        k.wait("tensor", l3)
        tr = k.op("tensor", lambda e: e.transpose(
            ps2[0:128, 0:33], small_st[0:33, :], id_f[0:33, 0:33]))
        k.wait("vector", tr)
        ev = k.op("vector", lambda e, kk=kk: e.tensor_copy(WBCD[:, kk, :],
                                                           ps2[:, 0:33]))
        k.wait("tensor", ev)
        k.wait("sync", ev)
    k.dma("sync", "msm2", lambda e: e.dma_start(brow_f[0:1, 0:16], b_B_d))
    k.dma("sync", "msm2", lambda e: e.dma_start(brow_f[0:1, 16:32], b_C_d))
    k.dma("sync", "msm2", lambda e: e.dma_start(brow_f[0:1, 32:33], b_Dl_d))
    b4l = k.dma("sync", "msm2", lambda e: e.dma_start(brow_f[0:1, 33:34],
                                                      bias_Dl_d))
    k.wait("vector", b4l)
    k.opf("vector", lambda e: e.tensor_add(brow_f[0:1, 32:33], brow_f[0:1, 32:33],
                                           brow_f[0:1, 33:34]))
    bb = k.op("vector", lambda e: e.tensor_copy(brow_b[0:1, :],
                                                brow_f[0:1, 0:33]))
    k.wait("tensor", bb)
    for tau in range(NT):
        for kk in range(NKC):
            cp = k.op("vector", lambda e, kk=kk, tau=tau: e.tensor_copy(
                xnb_st[:, :], hsl(kk, 128 * tau, 128, False)))
            k.wait("tensor", cp)
            mm = k.op("tensor", lambda e, kk=kk: e.matmul(
                ps2[:, 0:33], xnb_st[:, :], WBCD[:, kk, :],
                start=(kk == 0), stop=False))
            k.wait("vector", mm)
        bm = k.op("tensor", lambda e: e.matmul(
            ps2[:, 0:33], ones1_b[:, :], brow_b[:, :], start=False, stop=True))
        k.wait("scalar", bm)
        ev = k.op("scalar", lambda e, tau=tau: e.copy(BCD[:, tau, :],
                                                      ps2[:, 0:33]))
        k.wait("tensor", ev)
    dcol = BCD.ap()[:, :, 32:33].squeeze(2)
    k.wait("scalar", k.now("scalar"))
    k.op("scalar", lambda e: e.activation(sm1[:, :], dcol, AF.Abs))
    k.op("scalar", lambda e: e.activation(sm1[:, :], sm1[:, :], AF.Exp,
                                          scale=-1.0))
    t1 = k.op("scalar", lambda e: e.activation(sm1[:, :], sm1[:, :], AF.Ln,
                                               bias=one_col[:, :]))
    k.wait("vector", t1)
    k.opf("vector", lambda e: e.tensor_scalar(sm2[:, :], dcol, 0.0, None, ALU.max))
    spd = k.op("vector", lambda e: e.tensor_tensor(dcol, sm1[:, :], sm2[:, :],
                                                   ALU.add))
    k.wait("tensor", spd)
    for tau in range(NT):
        k.op("tensor", lambda e, tau=tau: e.transpose(
            ps2[0:16, 0:128], BCD.ap()[:, tau, 0:16], id_f[:, :]))
        t2 = k.op("tensor", lambda e, tau=tau: e.transpose(
            ps2[0:1, 128:256], BCD.ap()[:, tau, 32:33], id_f[:, :]))
        k.wait("vector", t2)
        k.op("vector", lambda e, tau=tau: e.tensor_copy(
            mv_bmdf[0:16, 128 * tau:128 * (tau + 1)], ps2[0:16, 0:128]))
        ev = k.op("vector", lambda e, tau=tau: e.tensor_copy(
            mv_bmdf[32:33, 128 * tau:128 * (tau + 1)], ps2[0:1, 128:256]))
        k.wait("tensor", ev)
    bmdv = k.now("vector")
    k.wait("sync", bmdv)
    k.dma("sync", "bmdst", (lambda e: e.dma_start(bmd_scr[0:16, :],
                                                  mv_bmdf[0:16, 0:T])))
    k.dma("sync", "bmdst", (lambda e: e.dma_start(bmd_scr[16:17, :],
                                                  mv_bmdf[32:33, 0:T])))
    d = dbg("bcd", (128, NT * 33))
    if d is not None:
        tok = k.now("vector")
        k.wait("sync", tok)
        k.dma("sync", "dbgd", (lambda e, d=d: e.dma_start(
            d, BCD.ap().rearrange("p a b -> p (a b)"))))
    k.barrier()

    scan_guard = {}
    mtr_guard = None
    k.wait("scalar", k.dtok("msm"))
    gp_prev = None
    for c in range(NMC):
        cc = slice(MC * c, MC * (c + 1))
        if gp_prev is not None:
            k.wait("sync", gp_prev)
        k.wait("sync", k.dtok("xnst"))
        xl = k.dma("sync", "mxn", (lambda e, cc=cc: e.dma_start(
            mv_xn, xn_scr[:, :, cc].transpose([1, 0, 2]))))
        k.op("vector", lambda e: e.memset(mv_y[:, :, :], 0.0))
        k.wait("sync", k.dtok("bmdst"))
        k.wait("sync", k.now("scalar"))
        dbl = k.dma("sync", "mbc", (lambda e, cc=cc: e.dma_start(
            mv_db, bmd_scr[16:17, cc].broadcast_to([128, MC]))))
        k.wait("scalar", dbl)
        k.wait("gpsimd", xl)
        for n in range(NST):
            k.wait("sync", k.now("gpsimd"))
            bml = k.dma("sync", "mbc", (lambda e, n=n, cc=cc: e.dma_start(
                mv_bm, bmd_scr[n:n + 1, cc].broadcast_to([128, MC]))))
            k.wait("gpsimd", bml)
            for dt in range(NKC):
                pp = dt % 2
                gidx = (c * NST + n) * NKC + dt
                if dt == 0:
                    vdrain = k.now("vector")
                    k.wait("scalar", vdrain)
                    k.wait("gpsimd", vdrain)
                elif gidx - 2 in scan_guard:
                    k.wait("scalar", scan_guard[gidx - 2])
                ab_t = k.op("scalar", lambda e, dt=dt, n=n, pp=pp:
                            e.activation(
                    mv_ab[pp], mv_db[:, :], AF.Exp,
                    scale=A_f.ap()[:, dt, n:n + 1]))
                k.wait("vector", ab_t)
                wt_t = k.op("vector", lambda e, dt=dt, n=n, pp=pp:
                            e.tensor_scalar(
                    mv_wt[pp], mv_ab[pp], 1.0,
                    invA_f.ap()[:, dt, n:n + 1],
                    ALU.subtract, ALU.mult))
                if dt >= 2:
                    k.wait("gpsimd", scan_guard[gidx - 2])
                bx_t = k.op("gpsimd", lambda e, dt=dt, pp=pp: e.tensor_tensor(
                    mv_bx[pp], mv_bm[:, :], mv_xn[:, dt, :], ALU.mult))
                k.wait("vector", wt_t)
                k.wait("vector", bx_t)
                k.op("vector", lambda e, pp=pp: e.tensor_tensor(
                    mv_xb[pp], mv_wt[pp], mv_bx[pp], ALU.mult))
                sc_t = k.op("vector", lambda e, dt=dt, n=n, pp=pp:
                            e.tensor_tensor_scan(
                    mv_h[:, dt, :], mv_ab[pp], mv_xb[pp],
                    st_cols.ap()[:, n, dt:dt + 1], ALU.mult, ALU.add))
                scan_guard[gidx] = sc_t
            k.wait("vector", sc_t)
            k.opf("vector", lambda e, n=n: e.tensor_copy(
                st_cols.ap()[:, n, :].rearrange("p (a b) -> p a b", b=1),
                mv_h[:, :, MC - 1:MC]))
            vtok = k.now("vector")
            k.wait("tensor", vtok)
            if mtr_guard is not None:
                k.wait("tensor", mtr_guard)
            ttok = None
            for dt in range(NKC):
                for j in range(MC // 128):
                    ttok = k.op("tensor", lambda e, dt=dt, j=j: e.matmul(
                        ps0[:, 1024 * j + 128 * dt:1024 * j + 128 * (dt + 1)],
                        mv_h[:, dt, 128 * j:128 * (j + 1)], id_b[:, :],
                        start=True, stop=True))
            k.wait("scalar", ttok)
            ev0 = None
            for j in range(MC // 128):
                ev0 = k.op("scalar", lambda e, j=j: e.copy(
                    mv_hT[:, j, :], ps0[:, 1024 * j:1024 * (j + 1)]))
            mtr_guard = ev0
            k.wait("vector", ev0)
            ylast = None
            for j in range(MC // 128):
                tau_g = (MC * c) // 128 + j
                ylast = k.op("vector", lambda e, j=j, n=n, tau_g=tau_g:
                             e.scalar_tensor_tensor(
                    mv_y[:, j, :], mv_hT[:, j, :],
                    BCD.ap()[:, tau_g, 16 + n:17 + n], mv_y[:, j, :],
                    ALU.mult, ALU.add))
        gp_prev = k.now("gpsimd")
        k.wait("sync", k.now("vector"))
        k.wait("sync", gp_prev)
        st1 = None
        for j in range(MC // 128):
            tau_g = (MC * c) // 128 + j
            st1 = k.dma("sync", "yst", (lambda e, j=j, tau_g=tau_g: e.dma_start(
                y_scr[tau_g, :, :], mv_y[:, j, :])))
        k.wait("vector", st1)
        k.wait("gpsimd", st1)
    d = dbg("stc", (128, NST * NKC))
    if d is not None:
        stok = k.now("vector")
        k.wait("sync", stok)
        sdmp = k.dma("sync", "dbgd", (lambda e, d=d: e.dma_start(
            d[:, :], st_cols.ap().rearrange("p a b -> p (a b)"))))
        k.wait("vector", sdmp)
    d = dbg("y", (T, D))
    if d is not None:
        k.wait("sync", k.dtok("yst"))
        for tau_g in range(NT):
            ldy = k.dma("sync", "dbgy", (lambda e, tau_g=tau_g: e.dma_start(
                io_a[:, :], y_scr[tau_g, :, :])))
            k.wait("sync", ldy)
            st9 = k.dma("sync", "dbgd", (lambda e, tau_g=tau_g, d=d: e.dma_start(
                d[128 * tau_g:128 * (tau_g + 1), :], io_a[:, :])))
            k.wait("sync", st9)
    k.barrier()

    d_att = dbg("att2", (8 * 128, 1024)) if "att2" in dbg_set else None
    for L in range(DEPTH6):
        for kk in range(NKC):
            k.dma("gpsimd", "wq", (lambda e, kk=kk, L=L: e.dma_start(
                wqkv_v[:, kk, :], Wqkv_d[L, 128 * kk:128 * (kk + 1), :])))
            k.dma("gpsimd", "wo", (lambda e, kk=kk, L=L: e.dma_start(
                wo_v[:, kk, :], Wo_d[L, 128 * kk:128 * (kk + 1), :])))
        wq_tok = k.dtok("wq")
        wo_tok = k.dtok("wo")
        k.dma("sync", "lsm", (lambda e, L=L: e.dma_start(
            small_st[0:8, :], lng_d[L].rearrange("(a b) -> a b", b=128))))
        k.dma("sync", "lsm", (lambda e, L=L: e.dma_start(
            small_st[8:16, :], lnb_d[L].rearrange("(a b) -> a b", b=128))))
        l3 = k.dma("sync", "lsm", (lambda e, L=L: e.dma_start(
            small_st[16:24, :], b2_d[L].rearrange("(a b) -> a b", b=128))))
        k.wait("tensor", l3)
        tr = k.op("tensor", lambda e: e.transpose(ps2[0:128, 0:24],
                                                  small_st[0:24, :],
                                                  id_f[0:24, 0:24]))
        k.wait("vector", tr)
        k.op("vector", lambda e: e.tensor_copy(lng_c[:, :], ps2[:, 0:8]))
        k.op("vector", lambda e: e.tensor_copy(lnb_c[:, :], ps2[:, 8:16]))
        ev = k.op("vector", lambda e: e.tensor_copy(b2_c[:, :], ps2[:, 16:24]))
        k.wait("tensor", ev)
        k.wait("sync", ev)
        for half, base in ((0, 0), (1, DI)):
            k.wait("vector", k.now("tensor"))
            ms = k.op("vector", lambda e: e.memset(small_st[0:32, :], 0.0))
            k.wait("sync", ms)
            k.dma("sync", "lsm", (lambda e, L=L, base=base: e.dma_start(
                small_st[0:21, :],
                b1_d[L, base:base + 21 * 128].rearrange("(a b) -> a b", b=128))))
            lb_ = k.dma("sync", "lsm", (lambda e, L=L, base=base: e.dma_start(
                small_st[21:22, 0:42], b1_d[L, base + 21 * 128:base + DI])))
            k.wait("tensor", lb_)
            tr = k.op("tensor", lambda e: e.transpose(
                ps2[0:128, 0:22], small_st[0:22, :], id_f[0:22, 0:22]))
            k.wait("vector", tr)
            ev = k.op("vector", lambda e, half=half: e.tensor_copy(
                b1_c[:, 22 * half:22 * (half + 1)], ps2[:, 0:22]))
            k.wait("tensor", ev)
            k.wait("sync", ev)
        lq = k.dma("sync", "lsm", (lambda e, L=L: e.dma_start(
            qs_rep[:, :], qs_d[L:L + 1, :].broadcast_to([128, DH]))))
        lk = k.dma("sync", "lsm", (lambda e, L=L: e.dma_start(
            ks_rep[:, :], ks_d[L:L + 1, :].broadcast_to([128, DH]))))
        k.wait("vector", lk)
        for hl in range(2):
            dsl = slice(32 * hl, 32 * (hl + 1))
            ssl = slice(32 * (1 - hl), 32 * (2 - hl))
            k.op("vector", lambda e, dsl=dsl, ssl=ssl: e.tensor_copy(
                qs_swp[:, dsl], qs_rep[:, ssl]))
            k.op("vector", lambda e, dsl=dsl, ssl=ssl: e.tensor_copy(
                ks_swp[:, dsl], ks_rep[:, ssl]))
        ev = k.now("vector")
        k.wait("tensor", ev)
        k.barrier()

        for kk in range(4, NKC):
            k.op("vector", lambda e, kk=kk: e.tensor_copy(
                hbnd[:, kk - 4, 0:15],
                bass.AP(h[kk], 128, [[2050, 128], [128, 15]])))
        hbnd_tok = k.now("vector")
        prev_t = None
        prev_v = None
        prev_ev = None
        prev_tab_rd = None
        prev_s = None
        for tau in range(NT):
            sl = tau % 2
            rs = tau % 3
            if tau >= 2:
                k.wait("vector", xsh_guard[tau - 2])
            for kk in range(NKC):
                if kk < 4 or tau == 0:
                    k.op("vector", lambda e, kk=kk, sl=sl, tau=tau: e.tensor_copy(
                        xsh[sl][:, kk, :], hsl(kk, 128 * tau, 128, True)))
                else:
                    k.op("vector", lambda e, kk=kk, sl=sl, tau=tau: e.tensor_copy(
                        xsh[sl][:, kk, 0:1], hbnd[:, kk - 4, tau - 1:tau]))
                    k.op("vector", lambda e, kk=kk, sl=sl, tau=tau: e.tensor_copy(
                        xsh[sl][:, kk, 1:128],
                        h[kk][:, 128 * tau + 1:128 * (tau + 1)]))
            xsh_tok = k.now("vector")
            if tau == 0:
                xsh_guard = {}
            k.wait("tensor", xsh_tok)
            k.wait("tensor", wq_tok)
            if prev_v is not None:
                k.wait("tensor", prev_v)
            if prev_ev is not None:
                k.wait("tensor", prev_ev)
            qkv_last = None
            for n in range(6):
                dst = (ps0.ap()[:, 512 * n:512 * (n + 1)] if n < 4 else
                       ps1.ap()[:, 512 * (n - 4):512 * (n - 3)])
                for kk in range(NKC):
                    qkv_last = k.op("tensor", lambda e, kk=kk, sl=sl, dst=dst,
                                    n=n: e.matmul(
                        dst, xsh[sl][:, kk, :],
                        wqkv_v[:, kk, 512 * n:512 * (n + 1)],
                        start=(kk == 0), stop=(kk == NKC - 1)))
            xsh_guard[tau] = qkv_last
            k.wait("scalar", qkv_last)
            if prev_t is not None:
                k.wait("scalar", prev_t)
            k.op("scalar", lambda e: e.copy(q_tok[:, :], ps0[:, 0:1024]))
            k.op("scalar", lambda e: e.copy(k_tok[:, :], ps0[:, 1024:2048]))
            vev = k.op("scalar", lambda e, rs=rs: e.copy(vring[rs][:, :],
                                                         ps1[:, 0:1024]))
            k.wait("tensor", vev)
            if d_att is not None and L == 0 and tau == 2:
                qk_ev2 = k.now("scalar")
                k.wait("sync", qk_ev2)
                dtk_ = k.dma("sync", "dbgd", (lambda e: e.dma_start(
                    d_att[896:1024, 0:1024], q_tok[:, :])))
                k.wait("scalar", dtk_)
                k.wait("vector", dtk_)
            if prev_tab_rd is not None:
                k.wait("sync", prev_tab_rd)
            tl = k.dma("sync", "tab", (lambda e, tau=tau: e.dma_start(
                tab4[:, :, :],
                HC["tabs"][:, 128 * tau:128 * (tau + 1), :].transpose([1, 0, 2]))))
            k.wait("vector", tl)
            k.op("vector", lambda e: e.tensor_tensor(tcmb[:, :], tab4[:, 0, :],
                                                     tab4[:, 2, :], ALU.mult))
            k.op("vector", lambda e: e.tensor_tensor(qcs[:, :], tcmb[:, :],
                                                     qs_rep[:, :], ALU.mult))
            k.op("vector", lambda e: e.tensor_tensor(tcmb[:, :], tab4[:, 1, :],
                                                     tab4[:, 2, :], ALU.mult))
            k.op("vector", lambda e: e.tensor_tensor(qss[:, :], tcmb[:, :],
                                                     qs_swp[:, :], ALU.mult))
            k.op("vector", lambda e: e.tensor_tensor(tcmb[:, :], tab4[:, 0, :],
                                                     tab4[:, 3, :], ALU.mult))
            k.op("vector", lambda e: e.tensor_tensor(kcs[:, :], tcmb[:, :],
                                                     ks_rep[:, :], ALU.mult))
            k.op("vector", lambda e: e.tensor_tensor(tcmb[:, :], tab4[:, 1, :],
                                                     tab4[:, 3, :], ALU.mult))
            tbk = k.op("vector", lambda e: e.tensor_tensor(
                kss[:, :], tcmb[:, :], ks_swp[:, :], ALU.mult))
            prev_tab_rd = tbk
            sq = k.op("scalar", lambda e: e.activation(qsq[:, :], q_tok[:, :],
                                                       AF.Square))
            k.wait("vector", sq)
            k.opf("vector", lambda e: e.reduce_sum(
                ssc.ap().rearrange("p (a b) -> p a b", b=1),
                qsq.ap().rearrange("p (g d) -> p g d", d=DH), axis=AX.X))
            rq = k.op("vector", lambda e: e.tensor_scalar(
                ssc[:, :], ssc[:, :], 1e-24, None, ALU.max))
            k.wait("scalar", rq)
            k.op("scalar", lambda e: e.activation(ssc[:, :], ssc[:, :], AF.Ln))
            rqe = k.op("scalar", lambda e: e.activation(rsq8[:, :], ssc[:, :],
                                                        AF.Exp, scale=-0.5))
            k.wait("vector", rqe)
            k.opf("vector", lambda e: e.tensor_scalar(rsq8[:, :], rsq8[:, :],
                                                      8.0, None, ALU.mult))
            k.op("vector", lambda e: e.tensor_tensor(
                v3(q_tok), v3(q_tok),
                rsq8.ap().unsqueeze(2).broadcast_to([128, HEADS, DH]),
                ALU.mult))
            k.op("vector", lambda e: e.tensor_tensor(
                v3(qp2), v3(q_tok), hb(qcs), ALU.mult))
            k.op("vector", lambda e: e.tensor_tensor(
                v4(qrot)[:, :, 0:1, :], v4(q_tok)[:, :, 1:2, :],
                hb2(qss, 0), ALU.mult))
            k.op("vector", lambda e: e.tensor_tensor(
                v4(qrot)[:, :, 1:2, :], v4(q_tok)[:, :, 0:1, :],
                hb2(qss, 1), ALU.mult))
            qdone = k.op("vector", lambda e: e.tensor_tensor(
                qrot[:, :], qrot[:, :], qp2[:, :], ALU.add))
            ks_ = k.op("scalar", lambda e: e.activation(ksq[:, :], k_tok[:, :],
                                                        AF.Square))
            k.wait("vector", ks_)
            k.opf("vector", lambda e: e.tensor_reduce(
                linv.ap().rearrange("p (a b) -> p a b", b=1),
                ksq.ap().rearrange("p (g d) -> p g d", d=DH), axis=AX.X,
                op=ALU.add))
            rk = k.op("vector", lambda e: e.tensor_scalar(
                linv[:, :], linv[:, :], 1e-24, None, ALU.max))
            k.wait("gpsimd", tbk)
            if prev_t is not None:
                k.wait("gpsimd", prev_t)
            k.wait("scalar", rk)
            k.op("scalar", lambda e: e.activation(linv[:, :], linv[:, :], AF.Ln))
            rke = k.op("scalar", lambda e: e.activation(linv[:, :], linv[:, :],
                                                        AF.Exp, scale=-0.5))
            k.wait("gpsimd", rke)
            k.op("gpsimd", lambda e: e.tensor_tensor(
                v3(k_tok), v3(k_tok),
                linv.ap().unsqueeze(2).broadcast_to([128, HEADS, DH]),
                ALU.mult))
            k.op("gpsimd", lambda e: e.tensor_tensor(
                v3(kp2), v3(k_tok), hb(kcs), ALU.mult))
            k.op("gpsimd", lambda e: e.tensor_tensor(
                v4(krot)[:, :, 0:1, :], v4(k_tok)[:, :, 1:2, :],
                hb2(kss, 0), ALU.mult))
            k.op("gpsimd", lambda e: e.tensor_tensor(
                v4(krot)[:, :, 1:2, :], v4(k_tok)[:, :, 0:1, :],
                hb2(kss, 1), ALU.mult))
            kdone = k.op("gpsimd", lambda e: e.tensor_tensor(
                krot[:, :], krot[:, :], kp2[:, :], ALU.add))
            if d_att is not None and L == 0 and tau == 2:
                k.wait("sync", qdone)
                k.wait("sync", kdone)
                k.dma("sync", "dbgd", (lambda e, sl=sl: e.dma_start(
                    d_att[384:512, 192:704],
                    xsh[sl].ap().rearrange("p a b -> p (a b)").bitcast(F32))))
                k.dma("sync", "dbgd", (lambda e: e.dma_start(
                    d_att[640:768, 512:768],
                    wqkv_v[:, 0, 0:512].bitcast(F32))))
                k.dma("sync", "dbgd", (lambda e: e.dma_start(
                    d_att[256:384, 64:128], qcs[:, :])))
                k.dma("sync", "dbgd", (lambda e: e.dma_start(
                    d_att[256:384, 128:192], qss[:, :])))
                k.dma("sync", "dbgd", (lambda e: e.dma_start(
                    d_att[256:384, 192:256], qs_rep[:, :])))
                k.dma("sync", "dbgd", (lambda e: e.dma_start(
                    d_att[256:384, 256:320], tab4[:, 0, :])))
                k.dma("sync", "dbgd", (lambda e: e.dma_start(
                    d_att[512:640, 16:528],
                    qp2.ap().bitcast(F32)[:, 0:512])))
                k.dma("sync", "dbgd", (lambda e: e.dma_start(
                    d_att[0:128, 0:512], qrot.ap().bitcast(F32))))
                k.dma("sync", "dbgd", (lambda e: e.dma_start(
                    d_att[128:256, 0:512], krot.ap().bitcast(F32))))
                dsm = k.dma("sync", "dbgd", (lambda e: e.dma_start(
                    d_att[256:384, 0:16], rsq8[:, :])))
                k.wait("vector", dsm)
            k.wait("tensor", qdone)
            k.wait("tensor", kdone)
            qtm = None
            for kk in range(NKC):
                qtm = k.op("tensor", lambda e, kk=kk: e.matmul(
                    ps0[:, 128 * kk:128 * (kk + 1)],
                    qrot[:, 128 * kk:128 * (kk + 1)], id_b[:, :],
                    start=True, stop=True))
            ktm = None
            for kk in range(NKC):
                ktm = k.op("tensor", lambda e, kk=kk: e.matmul(
                    ps1[:, 128 * kk:128 * (kk + 1)],
                    krot[:, 128 * kk:128 * (kk + 1)], id_b[:, :],
                    start=True, stop=True))
            k.wait("scalar", qtm)
            if prev_t is not None:
                k.wait("scalar", prev_t)
            qev = k.op("scalar", lambda e: e.copy(
                q_fT.ap().rearrange("p a b -> p (a b)"), ps0[:, 0:1024]))
            k.wait("scalar", ktm)
            kev = k.op("scalar", lambda e, rs=rs: e.copy(
                kring[rs].ap().rearrange("p a b -> p (a b)"), ps1[:, 0:1024]))
            k.wait("tensor", kev)
            tiles = []
            for dtk in (2, 1, 0):
                tkt = tau - dtk
                if tkt >= 0:
                    mask = "lo" if dtk == 2 else ("up" if dtk == 0 else None)
                    tiles.append((tkt, mask))
            Wd = 128 * len(tiles)
            ex_tok = {}
            ptm_tok = {}
            ev_tok = {}
            pv_tok = {}
            ntl = len(tiles)

            def do_pt(hd):
                hp2 = hd % 2
                s4 = hd % 4
                k.wait("tensor", ex_tok[hd])
                if hd >= 4:
                    k.wait("tensor", ev_tok[hd - 4])
                for ti in range(ntl):
                    k.op("tensor", lambda e, hp2=hp2, s4=s4, ti=ti: e.matmul(
                        ps0[:, 512 * s4 + 128 * ti:512 * s4 + 128 * (ti + 1)],
                        P_sb[hp2][:, 128 * ti:128 * (ti + 1)], id_b[:, :],
                        start=True, stop=True))
                ptm_tok[hd] = k.now("tensor")
                k.wait("scalar", ptm_tok[hd])
                if hd >= 2:
                    k.wait("scalar", pv_tok[hd - 2])
                ev_tok[hd] = k.op("scalar", lambda e, hp2=hp2, s4=s4: e.copy(
                    PT_sb[hp2].ap().rearrange("p a b -> p (a b)")[:, 0:128 * ntl],
                    ps0[:, 512 * s4:512 * s4 + 128 * ntl]))

            def do_pv(hd):
                hp2 = hd % 2
                k.wait("tensor", ev_tok[hd])
                for ti, (tkt, mask) in enumerate(tiles):
                    vrs = tkt % 3
                    startmm = (ti == 0 and (hd % 8) == 0)
                    lastmm = (ti == ntl - 1 and (hd % 8) == 7)
                    k.op("tensor", lambda e, hp2=hp2, ti=ti, hd=hd, vrs=vrs,
                         s=startmm, l=lastmm: e.matmul(
                        ps1[:, 64 * hd:64 * (hd + 1)], PT_sb[hp2][:, ti, :],
                        vring[vrs][:, 64 * hd:64 * (hd + 1)],
                        start=s, stop=l))
                pv_tok[hd] = k.now("tensor")

            for hd in range(HEADS):
                sps = ps2 if (hd % 2 == 0) else ps3
                hp = hd % 2
                if hd >= 2:
                    k.wait("tensor", ex_tok[hd - 2])
                first = True
                k.op("tensor", lambda e, sps=sps, Wd=Wd: e.matmul(
                    sps[:, 0:Wd], id_b[:, :], mk3_b[:, 384 - Wd:384],
                    start=True, stop=False))
                first = False
                for ti, (tkt, mask) in enumerate(tiles):
                    last = (ti == len(tiles) - 1)
                    krs = tkt % 3
                    po = 64 * (hd % 2)
                    k.op("tensor", lambda e, sps=sps, ti=ti, hd=hd, krs=krs,
                         f=first, l=last, po=po: e.matmul(
                        sps[:, 128 * ti:128 * (ti + 1)],
                        q_fT.ap()[po:po + 64, hd // 2, :],
                        kring[krs].ap()[po:po + 64, hd // 2, :],
                        start=f, stop=l))
                    first = False
                sc_tok = k.now("tensor")
                if hd >= 1:
                    do_pt(hd - 1)
                if hd >= 2:
                    do_pv(hd - 2)
                k.wait("vector", sc_tok)
                bv = k.op("vector", lambda e, sps=sps, hd=hd, Wd=Wd: e.reduce_max(
                    nmax[:, hd:hd + 1], sps[:, 0:Wd], axis=AX.X, negate=True))
                k.wait("scalar", bv)
                if hd >= 2:
                    k.wait("scalar", ptm_tok[hd - 2])
                ex_tok[hd] = k.op("scalar", lambda e, sps=sps, hd=hd, hp=hp,
                                  Wd=Wd: e.activation(
                    P_sb[hp][:, 0:Wd], sps[:, 0:Wd], AF.Exp,
                    bias=nmax[:, hd:hd + 1], scale=1.0,
                    accum_out=lsum[:, hd:hd + 1]))
                if d_att is not None and L == 0 and tau == 2 and hd == 0:
                    k.wait("sync", ex_tok[hd])
                    k.dma("sync", "dbgd", (lambda e: e.dma_start(
                        d_att[384:512, 0:192],
                        P_sb[0].ap()[:, 0:384].bitcast(F32)[:, 0:192])))
                    k.dma("sync", "dbgd", (lambda e: e.dma_start(
                        d_att[512:640, 0:16], lsum[:, :])))
            do_pt(HEADS - 1)
            do_pv(HEADS - 2)
            do_pv(HEADS - 1)
            prev_s = ex_tok[HEADS - 1]
            prev_ev = ev_tok[HEADS - 1]
            k.wait("vector", pv_tok[HEADS - 1])
            k.wait("vector", prev_s)
            k.opf("vector", lambda e: e.reciprocal(linv[:, :], lsum[:, :]))
            cv1 = k.op("vector", lambda e: e.tensor_tensor(
                ctx_tok.ap().rearrange("p (g d) -> p g d", d=DH),
                ps1.ap().rearrange("p (g d) -> p g d", d=DH),
                linv.ap().unsqueeze(2).broadcast_to([128, HEADS, DH]),
                ALU.mult))
            if d_att is not None and L == 0 and tau == 2:
                k.wait("sync", cv1)
                k.dma("sync", "dbgd", (lambda e: e.dma_start(
                    d_att[640:768, 0:512],
                    ctx_tok.ap().bitcast(F32)[:, 0:512])))
                k.dma("sync", "dbgd", (lambda e: e.dma_start(
                    d_att[768:896, 0:16], linv[:, :])))
            k.wait("tensor", cv1)
            k.wait("tensor", prev_ev)
            ctm = None
            for kk in range(NKC):
                ctm = k.op("tensor", lambda e, kk=kk: e.matmul(
                    ps0[:, 128 * kk:128 * (kk + 1)],
                    ctx_tok[:, 128 * kk:128 * (kk + 1)], id_b[:, :],
                    start=True, stop=True))
            k.wait("scalar", ctm)
            cev = k.op("scalar", lambda e: e.copy(
                ctx_fT.ap().rearrange("p a b -> p (a b)"), ps0[:, 0:1024]))
            prev_ev = cev
            k.wait("tensor", cev)
            k.wait("tensor", wo_tok)
            wo_ev = {}
            for m in range(NKC):
                spo = ps2 if (m % 2 == 0) else ps3
                if m >= 2:
                    k.wait("tensor", wo_ev[m - 2])
                wot = None
                for kk in range(NKC):
                    wot = k.op("tensor", lambda e, m=m, kk=kk, spo=spo: e.matmul(
                        spo[:, 0:128], wo_v[:, kk, 128 * m:128 * (m + 1)],
                        ctx_fT[:, kk, :], start=(kk == 0),
                        stop=(kk == NKC - 1)))
                k.wait("vector", wot)
                wo_ev[m] = k.op("vector", lambda e, m=m, spo=spo, tau=tau:
                                e.tensor_tensor(
                    h[m][:, 1 + 128 * tau:1 + 128 * (tau + 1)], spo[:, 0:128],
                    h[m][:, 1 + 128 * tau:1 + 128 * (tau + 1)], ALU.add))
            prev_t = k.now("tensor")
            prev_v = k.now("vector")
        dd = dbg(f"hA{L}", (NKC * 128, T))
        if dd is not None:
            tok = k.now("vector")
            k.wait("sync", tok)
            for i in range(NKC):
                k.dma("sync", "dbgd", (lambda e, i=i, dd=dd: e.dma_start(
                    dd[128 * i:128 * (i + 1), :], h[i][:, 1:1 + T])))
        for _e in Bld.ENG:
            if "dbgd" in k.dsem:
                k.wait(_e, k.dtok("dbgd"))
        k.barrier()

        for c2 in range(NC2):
            w = 128 if c2 < 21 else 42
            k.dma("gpsimd", "w2l", (lambda e, c2=c2, w=w, L=L: e.dma_start(
                w2_v[0:w, c2, :], W2_d[L, 128 * c2:128 * c2 + w, :])))
        w2_tok = k.dtok("w2l")
        w1_guard = {}
        gg_guard = {}
        w1idx = 0
        for half in range(2):
            tb0 = 1024 * half
            for q4 in range(4):
                t0 = tb0 + 256 * q4
                smm = None
                for kk in range(NKC):
                    cpa = k.op("vector", lambda e, kk=kk, t0=t0: e.tensor_copy(
                        xh_b[:, :], hsl(kk, t0, 256, True)))
                    k.wait("scalar", cpa)
                    cps = k.op("scalar", lambda e: e.activation(
                        sq_b[:, :], xh_b[:, :], AF.Square))
                    k.wait("tensor", cps)
                    k.op("tensor", lambda e, kk=kk: e.matmul(
                        ps2[:, 0:256], ones128_b[:, :], xh_b[:, :],
                        start=(kk == 0), stop=(kk == NKC - 1)))
                    smm = k.op("tensor", lambda e, kk=kk: e.matmul(
                        ps3[:, 0:256], ones128_b[:, :], sq_b[:, :],
                        start=(kk == 0), stop=(kk == NKC - 1)))
                    k.wait("vector", smm)
                    k.wait("scalar", smm)
                k.op("vector", lambda e: e.tensor_scalar(
                    stat_t[:, :], ps2[:, 0:256], 1.0 / D, None, ALU.mult))
                k.op("vector", lambda e: e.tensor_tensor(
                    stat_c[:, :], stat_t[:, :], stat_t[:, :], ALU.mult))
                va = k.op("vector", lambda e: e.scalar_tensor_tensor(
                    stat_a[:, :], ps3[:, 0:256], 1.0 / D, stat_c[:, :],
                    ALU.mult, ALU.subtract))
                k.wait("scalar", va)
                k.op("scalar", lambda e: e.activation(
                    stat_a[:, :], stat_a[:, :], AF.Ln, bias=eps_col[:, :]))
                sa = k.op("scalar", lambda e: e.activation(
                    stat_a[:, :], stat_a[:, :], AF.Exp, scale=-0.5))
                k.wait("vector", sa)
                cc_ = k.op("vector", lambda e: e.scalar_tensor_tensor(
                    stat_c[:, :], stat_t[:, :], -1.0, stat_a[:, :],
                    ALU.mult, ALU.mult))
                k.wait("tensor", cc_)
                for kk in range(NKC):
                    k.op("vector", lambda e, kk=kk, t0=t0: e.tensor_tensor(
                        stat_t[:, :], hsl(kk, t0, 256, True), stat_a[:, :],
                        ALU.mult))
                    k.op("vector", lambda e: e.tensor_add(
                        stat_t[:, :], stat_t[:, :], stat_c[:, :]))
                    k.op("vector", lambda e, kk=kk, t0=t0, tb0=tb0:
                         e.tensor_scalar(
                        xn2_v[:, kk, t0 - tb0:t0 - tb0 + 256], stat_t[:, :],
                        lng_c[:, kk:kk + 1], lnb_c[:, kk:kk + 1],
                        ALU.mult, ALU.add))
            xn2_tok = k.now("vector")
            k.wait("tensor", xn2_tok)
            k.wait("gpsimd", xn2_tok)
            for c2 in range(NC2):
                w = 128 if c2 < 21 else 42
                pp = c2 % 2
                if w1idx - 1 in w1_guard:
                    k.wait("gpsimd", w1_guard[w1idx - 1])
                k.dma("gpsimd", "w1l", (lambda e, c2=c2, w=w, L=L, pp=pp:
                                        e.dma_start(
                    w1a_s[pp].ap()[:, :, 0:w],
                    W1_d[L, :, 128 * c2:128 * c2 + w].rearrange(
                        "(kk p) n -> p kk n", p=128))))
                lg = k.dma("gpsimd", "w1l", (lambda e, c2=c2, w=w, L=L, pp=pp:
                                             e.dma_start(
                    w1g_s[pp].ap()[:, :, 0:w],
                    W1_d[L, :, DI + 128 * c2:DI + 128 * c2 + w].rearrange(
                        "(kk p) n -> p kk n", p=128))))
                k.wait("tensor", lg)
                if w1idx - 1 in gg_guard:
                    k.wait("tensor", gg_guard[w1idx - 1][0])
                    k.wait("tensor", gg_guard[w1idx - 1][1])
                wlast = None
                for sp in range(2):
                    for kk in range(NKC):
                        k.op("tensor", lambda e, kk=kk, sp=sp, w=w, pp=pp:
                             e.matmul(
                            ps0[0:w, 512 * sp:512 * (sp + 1)],
                            w1a_s[pp].ap()[:, kk, 0:w],
                            xn2_v[:, kk, 512 * sp:512 * (sp + 1)],
                            start=(kk == 0), stop=(kk == NKC - 1)))
                        wlast = k.op("tensor", lambda e, kk=kk, sp=sp, w=w,
                                     pp=pp: e.matmul(
                            ps0[0:w, 1024 + 512 * sp:1024 + 512 * (sp + 1)],
                            w1g_s[pp].ap()[:, kk, 0:w],
                            xn2_v[:, kk, 512 * sp:512 * (sp + 1)],
                            start=(kk == 0), stop=(kk == NKC - 1)))
                w1_guard[w1idx] = wlast
                k.wait("scalar", wlast)
                ge = k.op("scalar", lambda e, w=w, c2=c2: e.activation(
                    gelu_t[0:w, :], ps0[0:w, 1024:2048], AF.Gelu,
                    bias=b1_c[0:w, 22 + c2:23 + c2]))
                k.wait("vector", ge)
                pp2 = w1idx % 2
                if w1idx - 2 in gg_guard:
                    pass
                gv = k.op("vector", lambda e, w=w, c2=c2, pp2=pp2:
                          e.scalar_tensor_tensor(
                    ggs[pp2][0:w, :], ps0[0:w, 0:1024], b1_c[0:w, c2:c2 + 1],
                    gelu_t[0:w, :], ALU.add, ALU.mult))
                gg_guard[w1idx] = (gv, ge)
                k.wait("sync", gv)
                gst = k.dma("sync", "ggst", (lambda e, c2=c2, w=w, half=half,
                                             pp2=pp2: e.dma_start(
                    gg_scr[c2, 0:w, 1024 * half:1024 * (half + 1)],
                    ggs[pp2][0:w, :])))
                k.wait("vector", gst)
                w1idx += 1
        gg_tok = k.dtok("ggst")
        k.wait("tensor", w2_tok)
        k.wait("gpsimd", gg_tok)
        lastg = k.now("tensor")
        k.wait("tensor", gg_guard[w1idx - 1][0])
        k.wait("tensor", gg_guard[w1idx - 2][0])
        prev_tile_mm = None
        for q8 in range(8):
            t0 = 256 * q8
            if prev_tile_mm is not None:
                k.wait("gpsimd", prev_tile_mm)
            gg_ld = {}
            for c2 in range(NC2):
                w = 128 if c2 < 21 else 42
                pp = c2 % 2
                if c2 - 1 in gg_ld:
                    k.wait("gpsimd", gg_ld[c2 - 1])
                gl = k.dma("gpsimd", "ggld", (lambda e, c2=c2, w=w, t0=t0, pp=pp:
                                              e.dma_start(
                    gg_in[pp][0:w, :], gg_scr[c2, 0:w, t0:t0 + 256])))
                k.wait("tensor", gl)
                for m in range(NKC):
                    k.op("tensor", lambda e, c2=c2, m=m, w=w, pp=pp: e.matmul(
                        ps0[:, 256 * m:256 * (m + 1)],
                        w2_v[0:w, c2, 128 * m:128 * (m + 1)],
                        gg_in[pp][0:w, :],
                        start=(c2 == 0 and (m % 2) == 0),
                        stop=(c2 == NC2 - 1 and (m % 2) == 1)))
                gg_ld[c2] = k.now("tensor")
            w2_done = k.now("tensor")
            prev_tile_mm = w2_done
            k.wait("vector", w2_done)
            ev = None
            for m in range(NKC):
                ev = k.op("vector", lambda e, m=m, t0=t0: e.scalar_tensor_tensor(
                    h[m][:, 1 + t0:1 + t0 + 256], ps0[:, 256 * m:256 * (m + 1)],
                    b2_c[:, m:m + 1], h[m][:, 1 + t0:1 + t0 + 256],
                    ALU.add, ALU.add))
            k.wait("tensor", ev)
        dd = dbg(f"hF{L}", (NKC * 128, T))
        if dd is not None:
            tok = k.now("vector")
            k.wait("sync", tok)
            for i in range(NKC):
                k.dma("sync", "dbgd", (lambda e, i=i, dd=dd: e.dma_start(
                    dd[128 * i:128 * (i + 1), :], h[i][:, 1:1 + T])))
        for _e in Bld.ENG:
            if "dbgd" in k.dsem:
                k.wait(_e, k.dtok("dbgd"))
        k.barrier()

    out_tok = None
    fin = None
    for tau in range(NT):
        if fin is not None:
            k.wait("gpsimd", fin)
        k.wait("gpsimd", k.dtok("yst"))
        lx = k.dma("gpsimd", "fin", (lambda e, tau=tau: e.dma_start(
            io_a[:, :], x_d[128 * tau:128 * (tau + 1), :])))
        ly = k.dma("gpsimd", "fin", (lambda e, tau=tau: e.dma_start(
            io_b[:, :], y_scr[tau, :, :])))
        tt = None
        for kk in range(NKC):
            tt = k.op("tensor", lambda e, kk=kk, tau=tau: e.transpose(
                ps0[:, 128 * kk:128 * (kk + 1)],
                h[kk][:, 1 + 128 * tau:1 + 128 * (tau + 1)], id_f[:, :]))
        k.wait("vector", tt)
        k.wait("vector", lx)
        k.wait("vector", ly)
        k.op("vector", lambda e: e.tensor_add(io_a[:, :], io_a[:, :],
                                              io_b[:, :]))
        fin = k.op("vector", lambda e: e.tensor_add(io_c[:, :], ps0[:, 0:1024],
                                                    io_a[:, :]))
        k.wait("tensor", fin)
        k.wait("sync", fin)
        out_tok = k.dma("sync", "outst", (lambda e, tau=tau: e.dma_start(
            out_d[128 * tau:128 * (tau + 1), :], io_c[:, :])))
    k.wait("sync", out_tok)
    k.wait("vector", out_tok)

    k.finalize()
    nc._kernel_es = es
    return nc, dbgt


def kernel(**inputs):
    x_all = np.ascontiguousarray(inputs["L0_attn_sequence"])
    n_cores = x_all.shape[0]
    HCv = host_consts()

    nc, _ = build_program()

    base = {
        "A": np.ascontiguousarray(inputs["L0_attn_A"], dtype=np.float32),
        "W_B": np.ascontiguousarray(inputs["L0_attn_W_B"], dtype=np.float32),
        "b_B": np.ascontiguousarray(inputs["L0_attn_b_B"], dtype=np.float32),
        "W_C": np.ascontiguousarray(inputs["L0_attn_W_C"], dtype=np.float32),
        "b_C": np.ascontiguousarray(inputs["L0_attn_b_C"], dtype=np.float32),
        "W_Delta": np.ascontiguousarray(inputs["L0_attn_W_Delta"], dtype=np.float32),
        "b_Delta": np.ascontiguousarray(inputs["L0_attn_b_Delta"], dtype=np.float32),
        "bias_Delta": np.ascontiguousarray(inputs["L0_attn_bias_Delta"], dtype=np.float32),
        "Wqkv": np.ascontiguousarray(inputs["L0_ffn_Wqkv"], dtype=np.float32),
        "q_scale": np.ascontiguousarray(inputs["L0_ffn_q_scale"], dtype=np.float32),
        "k_scale": np.ascontiguousarray(inputs["L0_ffn_k_scale"], dtype=np.float32),
        "Wo": np.ascontiguousarray(inputs["L0_ffn_Wo"], dtype=np.float32),
        "ln_g": np.ascontiguousarray(inputs["L0_ffn_ln_g"], dtype=np.float32),
        "ln_b": np.ascontiguousarray(inputs["L0_ffn_ln_b"], dtype=np.float32),
        "W1": np.ascontiguousarray(inputs["L0_ffn_W1"], dtype=np.float32),
        "b1": np.ascontiguousarray(inputs["L0_ffn_b1"], dtype=np.float32),
        "W2": np.ascontiguousarray(inputs["L0_ffn_W2"], dtype=np.float32),
        "b2": np.ascontiguousarray(inputs["L0_ffn_b2"], dtype=np.float32),
    }
    for k2, v in HCv.items():
        base["c_" + k2] = v

    in_maps = []
    for c in range(n_cores):
        m = dict(base)
        m["x"] = np.ascontiguousarray(x_all[c], dtype=np.float32)
        in_maps.append(m)

    res = run_bass_kernel_spmd(nc, in_maps, core_ids=list(range(n_cores)))
    out = np.stack([np.asarray(r["out"], dtype=np.float32) for r in res.results])
    return out
